# version 18 (final)
# baseline (speedup 1.0000x reference)
import contextlib
import numpy as np
import ml_dtypes
import concourse.bass as bass
import concourse.mybir as mybir

F32 = mybir.dt.float32
BF16 = mybir.dt.bfloat16
AF = mybir.ActivationFunctionType
OP = mybir.AluOpType
AX = mybir.AxisListType

ENGINES = ("PE", "ACT", "DVE", "SP", "POOL")


def ap_box(ap):
    t = ap.tensor
    fs = 1
    for d in list(t.shape)[1:]:
        fs *= int(d)
    off = int(ap.offset)
    p0, f0 = off // fs, off % fs
    pext = fext = 0
    for step, count in [(int(s), int(c)) for (s, c) in ap.ap]:
        if count <= 1:
            continue
        r = step * (count - 1)
        if step % fs == 0 and step != 0:
            pext += r // fs
        else:
            fext += r
    return (p0, p0 + pext + 1, f0, f0 + fext + 1)


def boxes_overlap(a, b):
    return a[0] < b[1] and b[0] < a[1] and a[2] < b[3] and b[2] < a[3]


class KB:
    def __init__(self, nc, stack: contextlib.ExitStack):
        self.nc = nc
        self.st = stack
        self.eng = {
            "PE": nc.tensor, "ACT": nc.scalar, "DVE": nc.vector,
            "SP": nc.sync, "POOL": nc.gpsimd,
        }
        self.chain = {}
        self.cnt = {}
        for e in ENGINES:
            self.chain[e] = stack.enter_context(nc.semaphore(f"ch_{e}"))
            self.cnt[e] = 0
        self.dsem = {}
        self.dcnt = {}
        for e in ("SP", "ACT", "POOL", "DVE"):
            self.dsem[e] = stack.enter_context(nc.semaphore(f"dma_{e}"))
            self.dcnt[e] = 0
        self.reg = {}
        self.waited = {e: {} for e in ENGINES}
        self.sbuf_bytes = 0
        self._blk = None

    def sb(self, name, shape, dtype):
        h = self.st.enter_context(self.nc.sbuf_tensor(name, list(shape), dtype))
        self.sbuf_bytes += int(np.prod(shape[1:])) * 128 * mybir.dt.size(dtype)
        return h

    def ps(self, name, shape, dtype=F32):
        return self.st.enter_context(self.nc.psum_tensor(name, list(shape), dtype))

    @contextlib.contextmanager
    def block(self, name=None):
        with self.nc.Block(name) as blk:
            self._blk = blk
            self._fns = {}
            yield blk
            self._blk = None
        self.reg = {}
        self.waited = {e: {} for e in ENGINES}

    @contextlib.contextmanager
    def section(self, engine):
        dec = getattr(self._blk, {"PE": "tensor", "ACT": "scalar", "DVE": "vector",
                                  "SP": "sync", "POOL": "gpsimd"}[engine])
        holder = {}

        def body(eng):
            holder["eng"] = eng
            for fn in holder["fns"]:
                fn(eng)

        holder["fns"] = []
        self._cur_engine = engine
        self._cur_fns = holder["fns"]
        yield
        self._cur_engine = None
        self._cur_fns = None
        dec(body)

    def _tokens_for(self, aps, kinds):
        toks = []
        for ap, kind in zip(aps, kinds):
            name = ap.tensor.name
            ent = self.reg.get(name)
            if not ent:
                continue
            box = ap_box(ap)
            for b, tok in ent["w"]:
                if boxes_overlap(box, b):
                    toks.append(tok)
            if kind == "w":
                for b, tok in ent["r"]:
                    if boxes_overlap(box, b):
                        toks.append(tok)
        return toks

    def _record(self, aps, kinds, token):
        for ap, kind in zip(aps, kinds):
            name = ap.tensor.name
            ent = self.reg.setdefault(name, {"w": [], "r": []})
            box = ap_box(ap)
            if kind == "w":
                ent["w"] = [(b, t) for (b, t) in ent["w"]
                            if not (box[0] <= b[0] and b[1] <= box[1]
                                    and box[2] <= b[2] and b[3] <= box[3])]
                ent["w"].append((box, token))
                ent["r"] = [(b, t) for (b, t) in ent["r"]
                            if not (box[0] <= b[0] and b[1] <= box[1]
                                    and box[2] <= b[2] and b[3] <= box[3])]
                if len(ent["w"]) > 48:
                    ent["w"] = self._coarsen(ent["w"])
            else:
                ent["r"].append((box, token))
                if len(ent["r"]) > 48:
                    ent["r"] = self._coarsen(ent["r"])

    @staticmethod
    def _coarsen(lst):
        old, new = lst[:len(lst) // 2], lst[len(lst) // 2:]
        per_sem = {}
        cover = [min(b[0] for b, _ in old), max(b[1] for b, _ in old),
                 min(b[2] for b, _ in old), max(b[3] for b, _ in old)]
        for b, (sem, val) in old:
            key = id(sem)
            if key not in per_sem or per_sem[key][1][1] < val:
                per_sem[key] = (tuple(cover), (sem, val))
        return list(per_sem.values()) + new

    def _emit_waits(self, engine, toks, eng_obj):
        w = self.waited[engine]
        need = {}
        for sem, val in toks:
            key = id(sem)
            if w.get(key, 0) < val and need.get(key, (None, 0))[1] < val:
                need[key] = (sem, val)
        for sem, val in need.values():
            eng_obj.wait_ge(sem, val)
            w[id(sem)] = val

    def _finish(self, engine, ins, aps, kinds, n_inc=1, sem=None):
        if sem is None:
            sem = self.chain[engine]
            self.cnt[engine] += n_inc
            val = self.cnt[engine]
        else:
            raise RuntimeError
        ins.then_inc(sem, n_inc)
        token = (sem, val)
        self._record(aps, kinds, token)
        return token

    def _op(self, engine, fn, aps, kinds):
        toks = self._tokens_for(aps, kinds)

        def thunk(eng):
            self._emit_waits(engine, toks, eng)
            ins = fn(eng)
            sem = self.chain[engine]
            self.cnt[engine] += 1
            ins.then_inc(sem, 1)

        raise RuntimeError("use immediate mode")



class KBI:

    def __init__(self, nc, stack: contextlib.ExitStack):
        self.nc = nc
        self.st = stack
        self.engmap = {"PE": "tensor", "ACT": "scalar", "DVE": "vector",
                       "SP": "sync", "POOL": "gpsimd"}
        self.chain = {}
        self.cnt = {}
        for e in ENGINES:
            self.chain[e] = stack.enter_context(nc.semaphore(f"ch_{e}"))
            self.cnt[e] = 0
        self.DMA_POOL = 8
        self.dsem = {}
        self.dcnt = {}
        self.didx = {}
        for e in ("SP", "ACT", "POOL"):
            self.dsem[e] = [stack.enter_context(nc.semaphore(f"dma_{e}{i}"))
                            for i in range(self.DMA_POOL)]
            self.dcnt[e] = [0] * self.DMA_POOL
            self.didx[e] = 0
        self.reg = {}
        self.waited = {e: {} for e in ENGINES}
        self.sbuf_bytes = 0
        self._sections = None

    def _uname(self, name):
        if not hasattr(self, "_names"):
            self._names = {}
        n = self._names.get(name, 0)
        self._names[name] = n + 1
        return name if n == 0 else f"{name}_{n}"

    def sb(self, name, shape, dtype):
        h = self.st.enter_context(self.nc.sbuf_tensor(self._uname(name), list(shape), dtype))
        self.sbuf_bytes += int(np.prod(shape[1:])) * 128 * mybir.dt.size(dtype)
        return h

    def ps(self, name, shape, dtype=F32):
        return self.st.enter_context(self.nc.psum_tensor(self._uname(name), list(shape), dtype))

    @contextlib.contextmanager
    def block(self, name=None):
        self._ops = {e: [] for e in ENGINES}
        yield
        with self.nc.Block(name) as blk:
            for e in ENGINES:
                ops = self._ops[e]
                if not ops:
                    continue
                dec = getattr(blk, self.engmap[e])

                def body(eng, ops=ops):
                    for f in ops:
                        f(eng)
                dec(body)
        self._ops = None
        dma_ids = {id(s) for pool in self.dsem.values() for s in pool}
        nreg = {}
        for name_, ent in self.reg.items():
            w = [(b, t) for (b, t) in ent["w"] if id(t[0]) in dma_ids]
            r = [(b, t) for (b, t) in ent["r"] if id(t[0]) in dma_ids]
            if w or r:
                nreg[name_] = {"w": w, "r": r}
        self.reg = nreg
        self.waited = {e: {} for e in ENGINES}

    def emit(self, engine, fn, rd=(), wr=(), inc=1):
        rd = list(rd)
        wr = list(wr)
        aps = rd + wr
        kinds = ["r"] * len(rd) + ["w"] * len(wr)
        toks = self._tokens_for(aps, kinds)
        self.cnt[engine] += 1
        val = self.cnt[engine]
        sem = self.chain[engine]
        waits = self._dedup_waits(engine, toks)

        def thunk(eng, fn=fn, waits=waits, sem=sem):
            for s, v in waits:
                eng.wait_ge(s, v)
            ins = fn(eng)
            ins.then_inc(sem, 1)
        self._ops[engine].append(thunk)
        token = (sem, val)
        self._record(aps, kinds, token)
        return token

    def raw(self, engine, fn):
        self._ops[engine].append(fn)

    def drain_dmas(self):
        for qe in ("SP", "ACT", "POOL"):
            for i in range(self.DMA_POOL):
                c = self.dcnt[qe][i]
                if c:
                    self.wait_token(qe, (self.dsem[qe][i], c))

    def wait_token(self, engine, token):
        waits = self._dedup_waits(engine, [token])

        def thunk(eng, waits=waits):
            for s, v in waits:
                eng.wait_ge(s, v)
        self._ops[engine].append(thunk)

    def _dedup_waits(self, engine, toks):
        w = self.waited[engine]
        need = {}
        for sem, val in toks:
            key = id(sem)
            if w.get(key, 0) < val and need.get(key, (None, 0))[1] < val:
                need[key] = (sem, val)
        for key, (sem, val) in need.items():
            w[key] = val
        return list(need.values())

    _tokens_for = KB._tokens_for
    _record = KB._record
    _coarsen = staticmethod(KB._coarsen)

    def dma(self, qeng, out_ap, in_ap):
        rd, wr = [in_ap], [out_ap]
        aps = rd + wr
        kinds = ["r", "w"]
        toks = self._tokens_for(aps, kinds)
        i = self.didx[qeng]
        self.didx[qeng] = (i + 1) % self.DMA_POOL
        sem = self.dsem[qeng][i]
        prev = self.dcnt[qeng][i]
        if prev:
            toks = toks + [(sem, prev)]
        self.dcnt[qeng][i] = prev + 16
        val = prev + 16
        waits = self._dedup_waits(qeng, toks)

        def thunk(eng, waits=waits, sem=sem, out_ap=out_ap, in_ap=in_ap):
            for s, v in waits:
                eng.wait_ge(s, v)
            eng.dma_start(out_ap, in_ap).then_inc(sem, 16)
        self._ops[qeng].append(thunk)
        token = (sem, val)
        self._record(aps, kinds, token)
        return token

    def tt(self, engine, out, a, b, op):
        return self.emit(engine, lambda e: e.tensor_tensor(out, a, b, op),
                         rd=[a, b], wr=[out])

    def ts(self, engine, out, a, s1, s2, op0, op1=None, accum_out=None):
        rd = [a] + [x for x in (s1, s2) if isinstance(x, bass.AP)]
        wr = [out] + ([accum_out] if accum_out is not None else [])
        if op1 is None:
            return self.emit(engine, lambda e: e.tensor_scalar(out, a, s1, s2, op0, accum_out=accum_out),
                             rd=rd, wr=wr)
        return self.emit(engine, lambda e: e.tensor_scalar(out, a, s1, s2, op0, op1, accum_out=accum_out),
                         rd=rd, wr=wr)

    def stt(self, out, a, s, b, op0, op1):
        rd = [a, b] + ([s] if isinstance(s, bass.AP) else [])
        return self.emit("DVE", lambda e: e.scalar_tensor_tensor(out, a, s, b, op0, op1),
                         rd=rd, wr=[out])

    def cp(self, engine, out, a):
        if engine == "ACT":
            return self.emit(engine, lambda e: e.copy(out, a), rd=[a], wr=[out])
        return self.emit(engine, lambda e: e.tensor_copy(out, a), rd=[a], wr=[out])

    def act(self, out, a, func, bias=None, scale=None, accum_out=None):
        kw = {}
        if bias is not None:
            kw["bias"] = bias
        if scale is not None:
            kw["scale"] = scale
        if accum_out is not None:
            kw["accum_out"] = accum_out
        rd = [a] + [x for x in (bias, scale) if isinstance(x, bass.AP)]
        wr = [out] + ([accum_out] if accum_out is not None else [])
        return self.emit("ACT", lambda e: e.activation(out, a, func, **kw), rd=rd, wr=wr)

    def memset(self, engine, ap, value):
        return self.emit(engine, lambda e: e.memset(ap, value), rd=[], wr=[ap])

    def reduce(self, out, a, op, negate=False):
        return self.emit("DVE", lambda e: e.tensor_reduce(out, a, AX.X, op, negate=negate),
                         rd=[a], wr=[out])

    def recip(self, out, a):
        return self.emit("DVE", lambda e: e.reciprocal(out, a), rd=[a], wr=[out])

    def mm(self, passes, out):
        rd = []
        for l, r in passes:
            rd += [l, r]
        aps = rd + [out]
        kinds = ["r"] * len(rd) + ["w"]
        out_toks = self._tokens_for([out], ["w"])
        self.cnt["PE"] += 1
        val = self.cnt["PE"]
        sem = self.chain["PE"]
        pw = []
        for i, (l, r) in enumerate(passes):
            t = self._tokens_for([l, r], ["r", "r"])
            if i == 0:
                t = out_toks + t
            pw.append(self._dedup_waits("PE", t))
        n = len(passes)

        def thunk(eng, passes=passes, out=out, pw=pw, sem=sem, n=n):
            ins = None
            for i, (l, r) in enumerate(passes):
                for sv, v in pw[i]:
                    eng.wait_ge(sv, v)
                ins = eng.matmul(out, l, r, start=(i == 0), stop=(i == n - 1))
            ins.then_inc(sem, 1)
        self._ops["PE"].append(thunk)
        token = (sem, val)
        self._record(aps, kinds, token)
        return token

    def mm_acc(self, passes, out, start, stop):
        rd = []
        for l, r in passes:
            rd += [l, r]
        aps = rd + [out]
        kinds = ["r"] * len(rd) + ["w"]
        out_toks = self._tokens_for([out], ["w"])
        self.cnt["PE"] += 1
        val = self.cnt["PE"]
        sem = self.chain["PE"]
        pw = []
        for i, (l, r) in enumerate(passes):
            t = self._tokens_for([l, r], ["r", "r"])
            if i == 0:
                t = out_toks + t
            pw.append(self._dedup_waits("PE", t))
        n = len(passes)

        def thunk(eng, passes=passes, out=out, pw=pw, sem=sem, n=n):
            ins = None
            for i, (l, r) in enumerate(passes):
                for sv, v in pw[i]:
                    eng.wait_ge(sv, v)
                ins = eng.matmul(out, l, r, start=(start and i == 0),
                                 stop=(stop and i == n - 1))
            ins.then_inc(sem, 1)
        self._ops["PE"].append(thunk)
        token = (sem, val)
        self._record(aps, kinds, token)
        return token


def hi_lo_split(kb, h_ap, l_ap, f32_ap, eng="DVE"):
    kb.cp(eng, h_ap, f32_ap)
    kb.tt(eng, l_ap, f32_ap, h_ap, OP.subtract)


import contextlib
import numpy as np
import ml_dtypes
import concourse.bass as bass
import concourse.mybir as mybir

BF = ml_dtypes.bfloat16
STATS_FOLD = False
INV_D = 1.0 / 1024.0

SL_B1A = 0
SL_B1G = 22
SL_B2 = 44
SL_LNG = 52
SL_LNB = 60
SL_QS = 68
SL_KS = 69
SL_EPS = 70
SL_NV = 80


class Geo:
    D = 1024
    FC = 8
    H = 16
    DH = 64
    WIN = 256
    DI = 2730
    NHC = 22
    LN_EPS = 1e-5

    def __init__(self, W=3072, margin=272, tile=512, n_layers=6, st=None):
        self.W = W
        self.M = margin
        self.TT = tile
        self.NL = n_layers
        self.st = st if st is not None else [0] * n_layers

    def tiles(self, li):
        out = []
        c = self.st[li]
        while c < self.W:
            e = min(c + self.TT, self.W)
            out.append((c, e))
            c = e
        return out


def hl(x):
    h = np.asarray(x, np.float32).astype(BF)
    l = (np.asarray(x, np.float32) - h.astype(np.float32)).astype(BF)
    return h, l


def fm(x):
    T, D = x.shape
    return np.ascontiguousarray(x.T.reshape(D // 128, 128, T).transpose(1, 0, 2))


def unfm(x):
    p, fc, T = x.shape
    return np.ascontiguousarray(x.transpose(1, 0, 2).reshape(fc * 128, T).T)


class Bufs:
    pass


def alloc_wbufs(kb):
    bw = Bufs()
    bw.wh = [kb.sb(f"wh{i}", [128, 24, 128], BF16) for i in range(2)]
    bw.wl = [kb.sb(f"wl{i}", [128, 24, 128], BF16) for i in range(2)]
    bw.w32 = [kb.sb(f"w32_{i}", [128, 4, 128], F32) for i in range(2)]
    bw.w32i = [0]
    return bw


def alloc_inner_bufs(kb, g: Geo, bw=None):
    b = Bufs()
    TT = g.TT
    if bw is not None:
        b.wh, b.wl, b.w32 = bw.wh, bw.wl, bw.w32
        b.w32i = bw.w32i
    b.yh = kb.sb("yh", [128, g.FC, TT], BF16)
    b.yl = kb.sb("yl", [128, g.FC, TT], BF16)
    b.th = kb.sb("th", [128, g.FC, TT], BF16)
    b.tl = kb.sb("tl", [128, g.FC, TT], BF16)
    b.hsv = kb.sb("hsv", [128, 2, g.FC], BF16)
    b.hsv2 = kb.sb("hsv2", [128, 2, g.FC], BF16)
    b.hsnap = kb.sb("hsnap", [128, 2, g.FC], BF16)
    if bw is None:
        b.wh = [kb.sb(f"wh{i}", [128, 24, 128], BF16) for i in range(2)]
        b.wl = [kb.sb(f"wl{i}", [128, 24, 128], BF16) for i in range(2)]
        b.w32 = [kb.sb(f"w32_{i}", [128, 4, 128], F32) for i in range(2)]
        b.w32i = [0]
    b.t32 = kb.sb("t32", [128, TT], F32)
    b.sq32 = kb.sb("sq32", [128, TT], F32)
    kb._shared_t32 = b.t32
    kb._shared_sq32 = b.sq32
    b.sqh = kb.sb("sqh", [128, TT], BF16)
    b.sql = kb.sb("sql", [128, TT], BF16)
    b.muR = kb.sb("muR", [128, TT], F32)
    kb._shared_muR = b.muR
    b.m2 = kb.sb("m2", [128, TT], F32)
    b.varf = b.m2
    b.sdf = b.m2
    b.rR = kb.sb("rR", [128, TT], F32)
    b.u32 = kb.sb("u32", [128, TT], F32)
    b.x32 = b.u32
    kb._shared_u32 = b.u32
    b.gth = kb.sb("gth", [128, 11, TT], BF16)
    b.gtl = kb.sb("gtl", [128, 11, TT], BF16)
    b.w2acc = kb.sb("w2acc", [128, 4, TT], F32)
    b.ge32 = kb.sb("ge32", [128, TT], F32)
    b.gg32 = b.ge32
    b.y32 = kb.sb("y32", [128, TT], F32)
    b.r132 = b.y32
    kb._shared_y32 = b.y32
    b.ones128 = kb.sb("ones128", [128, 128], BF16)
    b.lvec = kb.sb("lvec", [128, SL_NV], F32)
    return b


def t_view(b, g, c, col, T, which):
    y = b.yh if which == "h" else b.yl
    off = col - 1 if c >= 4 else col
    return y[:, c, off:off + T]


def emit_build_t(kb, b, g, T, hsv, c_from=0):
    for c in range(c_from, g.FC):
        if c < 4:
            kb.cp("DVE", b.th[:, c, 0:T], b.yh[:, c, 0:T])
            kb.cp("DVE", b.tl[:, c, 0:T], b.yl[:, c, 0:T])
        else:
            kb.cp("DVE", b.th[:, c, 0:1], hsv[:, 0, c:c + 1])
            kb.cp("DVE", b.tl[:, c, 0:1], hsv[:, 1, c:c + 1])
            kb.cp("DVE", b.th[:, c, 1:T], b.yh[:, c, 0:T - 1])
            kb.cp("DVE", b.tl[:, c, 1:T], b.yl[:, c, 0:T - 1])
    for c in range(c_from, g.FC):
        kb.cp("DVE", hsv[:, 0, c:c + 1], b.yh[:, c, T - 1:T])
        kb.cp("DVE", hsv[:, 1, c:c + 1], b.yl[:, c, T - 1:T])
    if getattr(b, "cm_col", None) is not None:
        kb.ts("DVE", hsv[:, :, :], hsv[:, :, :], b.cm_col, None, OP.mult)


def emit_hsv_init(kb, b, g, hsv):
    kb.cp("DVE", hsv[:, :, :], b.hsnap[:, :, :])


def emit_ln_stats(kb, b, g, ps, T, src_h, src_l, eps=1e-5):
    s1 = ps["E"]
    s2 = ps["F"]
    passes = []
    for c in range(g.FC):
        passes += [(b.ones128[:, :], src_h(c)), (b.ones128[:, :], src_l(c))]
    kb.mm(passes, s1[:, 0:T])
    for c in range(g.FC):
        kb.tt("DVE", b.t32[:, 0:T], src_h(c), src_l(c), OP.add)
        kb.act(b.sq32[:, 0:T], b.t32[:, 0:T], AF.Square)
        kb.cp("DVE", b.sqh[:, 0:T], b.sq32[:, 0:T])
        kb.tt("DVE", b.sql[:, 0:T], b.sq32[:, 0:T], b.sqh[:, 0:T], OP.subtract)
        kb.mm_acc([(b.ones128[:, :], b.sqh[:, 0:T]),
                   (b.ones128[:, :], b.sql[:, 0:T])],
                  s2[:, 0:T], start=(c == 0), stop=(c == g.FC - 1))
    emit_ln_finalize(kb, b, g, ps, T)


def emit_ln_finalize(kb, b, g, ps, T):
    kb.ts("DVE", b.muR[:, 0:T], ps["E"][:, 0:T], INV_D, None, OP.mult)
    kb.tt("DVE", b.m2[:, 0:T], b.muR[:, 0:T], b.muR[:, 0:T], OP.mult)
    kb.stt(b.varf[:, 0:T], ps["F"][:, 0:T], INV_D, b.m2[:, 0:T],
           OP.mult, OP.subtract)
    kb.act(b.sdf[:, 0:T], b.varf[:, 0:T], AF.Sqrt,
           bias=b.lvec[:, SL_EPS:SL_EPS + 1])
    kb.recip(b.rR[:, 0:T], b.sdf[:, 0:T])


def emit_ffn_sublayer(kb, b, g, ps, li, wd, c0, c1, pre_done=False):
    T = c1 - c0
    lv = b.lvec

    if pre_done:
        if getattr(b, "cm_col", None) is not None:
            kb.ts("DVE", b.hsv2[:, :, :], b.hsv2[:, :, :], b.cm_col, None,
                  OP.mult)
        emit_ln_finalize(kb, b, g, ps, T)
    else:
        emit_build_t(kb, b, g, T, b.hsv2)
        emit_ln_stats(kb, b, g, ps, T,
                      lambda c: b.th[:, c, 0:T], lambda c: b.tl[:, c, 0:T],
                      eps=g.LN_EPS)
    for c in range(g.FC):
        kb.tt("DVE", b.t32[:, 0:T], b.th[:, c, 0:T], b.tl[:, c, 0:T], OP.add)
        kb.tt("DVE", b.u32[:, 0:T], b.t32[:, 0:T], b.muR[:, 0:T], OP.subtract)
        kb.tt("DVE", b.x32[:, 0:T], b.u32[:, 0:T], b.rR[:, 0:T], OP.mult)
        kb.ts("DVE", b.x32[:, 0:T], b.x32[:, 0:T],
              lv[:, SL_LNG + c:SL_LNG + c + 1], lv[:, SL_LNB + c:SL_LNB + c + 1],
              OP.mult, OP.add)
        kb.cp("ACT", b.th[:, c, 0:T], b.x32[:, 0:T])
        kb.tt("DVE", b.tl[:, c, 0:T], b.x32[:, 0:T], b.th[:, c, 0:T], OP.subtract)

    for grp in range(2):
        for j in range(11):
            hc = 11 * grp + j
            rows = min(128, g.DI - 128 * hc)
            bi = j % 2
            pA = ps["A"] if j % 2 == 0 else ps["G"]
            pB = ps["B"] if j % 2 == 0 else ps["H"]
            emit_w_load_pre(kb, b, wd, bi, 0, WS_W1 + 16 * hc, 16)
            pa = [(b.wh[bi][:, kc, 0:rows], b.th[:, kc, 0:T]) for kc in range(8)]
            pa += [(b.wh[bi][:, kc, 0:rows], b.tl[:, kc, 0:T]) for kc in range(8)]
            pa += [(b.wl[bi][:, kc, 0:rows], b.th[:, kc, 0:T]) for kc in range(8)]
            kb.mm(pa, pA[0:rows, 0:T])
            pg = [(b.wh[bi][:, 8 + kc, 0:rows], b.th[:, kc, 0:T]) for kc in range(8)]
            pg += [(b.wh[bi][:, 8 + kc, 0:rows], b.tl[:, kc, 0:T]) for kc in range(8)]
            pg += [(b.wl[bi][:, 8 + kc, 0:rows], b.th[:, kc, 0:T]) for kc in range(8)]
            kb.mm(pg, pB[0:rows, 0:T])
            kb.act(b.ge32[0:rows, 0:T], pB[0:rows, 0:T], AF.Gelu,
                   bias=lv[0:rows, SL_B1G + hc:SL_B1G + hc + 1])
            kb.stt(b.gg32[0:rows, 0:T], pA[0:rows, 0:T],
                   lv[0:rows, SL_B1A + hc:SL_B1A + hc + 1], b.ge32[0:rows, 0:T],
                   OP.add, OP.mult)
            kb.cp("ACT", b.gth[0:rows, j, 0:T], b.gg32[0:rows, 0:T])
            kb.tt("DVE", b.gtl[0:rows, j, 0:T], b.gg32[0:rows, 0:T],
                  b.gth[0:rows, j, 0:T], OP.subtract)
        obank = [ps["C"], ps["D"], ps["E"], ps["F"]]
        for oc in range(g.FC):
            bi = oc % 2
            emit_w_load_pre(kb, b, wd, bi, 0, WS_W2 + 88 * grp + 11 * oc, 11)
            pw = []
            for j in range(11):
                hc = 11 * grp + j
                rows = min(128, g.DI - 128 * hc)
                pw += [(b.wh[bi][0:rows, j, :], b.gth[0:rows, j, 0:T]),
                       (b.wh[bi][0:rows, j, :], b.gtl[0:rows, j, 0:T]),
                       (b.wl[bi][0:rows, j, :], b.gth[0:rows, j, 0:T])]
            if oc < 4:
                kb.mm_acc(pw, obank[oc][:, 0:T], start=(grp == 0), stop=(grp == 1))
                if grp == 0:
                    continue
                src_ps = obank[oc]
                kb.stt(b.r132[:, 0:T], src_ps[:, 0:T],
                       lv[:, SL_B2 + oc:SL_B2 + oc + 1], b.yh[:, oc, 0:T],
                       OP.add, OP.add)
            else:
                pX = [ps["A"], ps["B"], ps["G"], ps["H"]][oc - 4]
                kb.mm(pw, pX[:, 0:T])
                if grp == 0:
                    kb.cp("ACT", b.w2acc[:, oc - 4, 0:T], pX[:, 0:T])
                    continue
                kb.tt("DVE", b.r132[:, 0:T], pX[:, 0:T],
                      b.w2acc[:, oc - 4, 0:T], OP.add)
                kb.stt(b.r132[:, 0:T], b.r132[:, 0:T],
                       lv[:, SL_B2 + oc:SL_B2 + oc + 1], b.yh[:, oc, 0:T],
                       OP.add, OP.add)
            kb.tt("DVE", b.y32[:, 0:T], b.r132[:, 0:T],
                  b.yl[:, oc, 0:T], OP.add)
            kb.cp("ACT", b.yh[:, oc, 0:T], b.y32[:, 0:T])
            kb.tt("DVE", b.yl[:, oc, 0:T], b.y32[:, 0:T],
                  b.yh[:, oc, 0:T], OP.subtract)


def emit_w_load(kb, b, g, wdram, o0, ow, bi, slot0=0, kcs=8):
    src = wdram[:, o0:o0 + ow].rearrange("(c p) o -> p c o", p=128)
    kb.dma("POOL", b.wh[bi][:, slot0:slot0 + kcs, 0:ow], src)
    for s0 in range(0, kcs, 4):
        n = min(4, kcs - s0)
        wi = b.w32i[0] % 2
        b.w32i[0] += 1
        kb.dma("SP", b.w32[wi][:, 0:n, 0:ow], src[:, s0:s0 + n, :])
        kb.tt("DVE", b.wl[bi][:, slot0 + s0:slot0 + s0 + n, 0:ow],
              b.w32[wi][:, 0:n, 0:ow],
              b.wh[bi][:, slot0 + s0:slot0 + s0 + n, 0:ow], OP.subtract)


def emit_w2_load(kb, b, g, w2dram, oc, bi, grp):
    r0 = 11 * 128 * grp
    nfull = 11 if grp == 0 else 10
    src = w2dram[r0:r0 + nfull * 128, 128 * oc:128 * (oc + 1)].rearrange(
        "(c p) o -> p c o", p=128)
    kb.dma("POOL", b.wh[bi][:, 0:nfull, :], src)
    for s0 in range(0, nfull, 4):
        n = min(4, nfull - s0)
        wi = b.w32i[0] % 2
        b.w32i[0] += 1
        kb.dma("SP", b.w32[wi][:, 0:n, :], src[:, s0:s0 + n, :])
        kb.tt("DVE", b.wl[bi][:, s0:s0 + n, :], b.w32[wi][:, 0:n, :],
              b.wh[bi][:, s0:s0 + n, :], OP.subtract)
    if grp == 1:
        src2 = w2dram[r0 + 10 * 128:g.DI, 128 * oc:128 * (oc + 1)]
        wi = b.w32i[0] % 2
        b.w32i[0] += 1
        kb.dma("SP", b.w32[wi][0:42, 0, :], src2)
        kb.dma("POOL", b.wh[bi][0:42, 10, :], src2)
        kb.tt("DVE", b.wl[bi][0:42, 10, :], b.w32[wi][0:42, 0, :],
              b.wh[bi][0:42, 10, :], OP.subtract)


SL_TINY = 71

WS_QKV = 0
WS_WO = 192
WS_W1 = 256
WS_W2 = 608
WS_N = 784


def alloc_wsplit_bufs(kb):
    sp = Bufs()
    sp.s32 = [kb.sb(f"wsp32_{i}", [128, 4, 128], F32) for i in range(2)]
    sp.sh = kb.sb("wsph", [128, 4, 4, 128], BF16)
    sp.sl = kb.sb("wspl", [128, 4, 4, 128], BF16)
    sp.i32 = [0]
    return sp


def _ws_chunk(kb, sp, src, n, ow, dsth, dstl):
    i = sp.i32[0] % 2
    j = sp.i32[0] % 4
    sp.i32[0] += 1
    kb.dma("SP", sp.s32[i][:, 0:n, 0:ow], src)
    kb.cp("ACT", sp.sh[:, j, 0:n, 0:ow], sp.s32[i][:, 0:n, 0:ow])
    kb.tt("DVE", sp.sl[:, j, 0:n, 0:ow], sp.s32[i][:, 0:n, 0:ow],
          sp.sh[:, j, 0:n, 0:ow], OP.subtract)
    kb.dma("ACT", dsth, sp.sh[:, j, 0:n, :])
    kb.dma("POOL", dstl, sp.sl[:, j, 0:n, :])


def _ws_chunk1(kb, sp, src2d, np_, ow, dsth, dstl):
    i = sp.i32[0] % 2
    j = sp.i32[0] % 4
    sp.i32[0] += 1
    kb.dma("SP", sp.s32[i][0:np_, 0, 0:ow], src2d)
    kb.cp("ACT", sp.sh[0:np_, j, 0, 0:ow], sp.s32[i][0:np_, 0, 0:ow])
    kb.tt("DVE", sp.sl[0:np_, j, 0, 0:ow], sp.s32[i][0:np_, 0, 0:ow],
          sp.sh[0:np_, j, 0, 0:ow], OP.subtract)
    kb.dma("ACT", dsth, sp.sh[:, j, 0, :])
    kb.dma("POOL", dstl, sp.sl[:, j, 0, :])


def _ws_run(kb, sp, whd_l, wld_l, wsrc, o0, ow, d0):
    nsl = wsrc.shape[0] // 128
    src = wsrc[:, o0:o0 + ow].rearrange("(c p) o -> p c o", p=128)
    for s0 in range(0, nsl, 4):
        n = min(4, nsl - s0)
        d = d0 + s0
        _ws_chunk(kb, sp, src[:, s0:s0 + n, 0:ow], n, ow,
                  whd_l[:, d:d + n, :], wld_l[:, d:d + n, :])


def ws_layer_thunks(kb, sp, g, fqkv_l, fwo_l, fw1_l, fw2_l, whd_l, wld_l):
    th = []

    def run(wsrc, o0, ow, d0):
        nsl = wsrc.shape[0] // 128
        src = wsrc[:, o0:o0 + ow].rearrange("(c p) o -> p c o", p=128)
        for s0 in range(0, nsl, 4):
            n = min(4, nsl - s0)
            d = d0 + s0
            th.append(lambda sv=src[:, s0:s0 + n, 0:ow], n=n, ow=ow,
                      dh=whd_l[:, d:d + n, :], dl=wld_l[:, d:d + n, :]:
                      _ws_chunk(kb, sp, sv, n, ow, dh, dl))

    for hp in range(8):
        for p3 in range(3):
            run(fqkv_l, 1024 * p3 + 128 * hp, 128, WS_QKV + 24 * hp + 8 * p3)
    for oc in range(8):
        run(fwo_l, 128 * oc, 128, WS_WO + 8 * oc)
    for hc in range(22):
        rows = min(128, g.DI - 128 * hc)
        run(fw1_l, 128 * hc, rows, WS_W1 + 16 * hc)
        run(fw1_l, g.DI + 128 * hc, rows, WS_W1 + 16 * hc + 8)
    for grp in range(2):
        for oc in range(8):
            r0 = 11 * 128 * grp
            nf = 11 if grp == 0 else 10
            src = fw2_l[r0:r0 + nf * 128,
                        128 * oc:128 * (oc + 1)].rearrange(
                "(c p) o -> p c o", p=128)
            d0 = WS_W2 + 88 * grp + 11 * oc
            for s0 in range(0, nf, 4):
                n = min(4, nf - s0)
                th.append(lambda sv=src[:, s0:s0 + n, :], n=n,
                          dh=whd_l[:, d0 + s0:d0 + s0 + n, :],
                          dl=wld_l[:, d0 + s0:d0 + s0 + n, :]:
                          _ws_chunk(kb, sp, sv, n, 128, dh, dl))
            if grp == 1:
                src2 = fw2_l[r0 + 10 * 128:g.DI, 128 * oc:128 * (oc + 1)]
                th.append(lambda sv=src2, dh=whd_l[:, d0 + 10, :],
                          dl=wld_l[:, d0 + 10, :]:
                          _ws_chunk1(kb, sp, sv, 42, 128, dh, dl))
    return th


def emit_weight_split(kb, sp, g, fqkv_l, fwo_l, fw1_l, fw2_l, whd_l, wld_l):
    for t in ws_layer_thunks(kb, sp, g, fqkv_l, fwo_l, fw1_l, fw2_l,
                             whd_l, wld_l):
        t()


def emit_w_load_pre(kb, b, wd, bi, slot0, s0, nsl):
    kb.dma("SP", b.wh[bi][:, slot0:slot0 + nsl, :],
           wd["whd"][:, s0:s0 + nsl, :])
    kb.dma("POOL", b.wl[bi][:, slot0:slot0 + nsl, :],
           wd["wld"][:, s0:s0 + nsl, :])


def alloc_attn_bufs(kb, g: Geo):
    a = Bufs()
    TT = g.TT
    NTC = (256 + TT) // 128
    a.khl = kb.sb("khl", [128, g.H, 256 + TT], BF16)
    a.vbh = kb.sb("vbh", [128, NTC, 1040], BF16)
    a.vbl = kb.sb("vbl", [128, NTC, 1040], BF16)
    a.qhh = kb.sb("qhh", [128, 2, TT], BF16)
    a.qll = kb.sb("qll", [128, 2, TT], BF16)
    a.q32 = kb._shared_sq32
    a.t1 = kb._shared_t32
    a.wv = kb._shared_u32
    a.qp32 = kb._shared_y32
    a.qf32 = a.qp32
    a.rqk = kb.sb("arqk", [4, TT], F32)
    a.sdq = a.rqk
    a.p32 = kb.sb("ap32", [128, 384], F32)
    a.ph = kb.sb("aph", [128, 2, 384], BF16)
    a.pl = kb.sb("apl", [128, 2, 384], BF16)
    a.aoh = kb.sb("aaoh", [128, 8, TT], BF16)
    a.aol = kb.sb("aaol", [128, 8, TT], BF16)
    a.rott = kb.sb("arott", [128, 4, TT], F32)
    a.msk = kb.sb("amsk", [128, g.TT // 128, 3, 128], BF16)
    a.colsel = kb.sb("acolsel", [128, 2], BF16)
    a.selpair = kb.sb("aselpair", [2, 128], F32)
    a.ones65 = kb.sb("aones65", [65, 64], F32)
    a.ident = kb.sb("aident", [128, 128], BF16)
    a.pbf = kb.sb("apbf", [128, 6, 128], BF16)
    a.pf32 = kb.sb("apf32", [128, 4, 128], F32)
    a.qhb = kb.sb("aqhb", [128, TT], BF16)
    a.qlb = kb.sb("aqlb", [128, TT], BF16)
    a.invd = kb.sb("ainvd", [65, 2, TT], F32)
    a.an32 = kb._shared_muR[0:64, :]
    return a


def emit_attn_consts(kb, b, a, g, cbf, cf32, pbf, pf32):
    kb.dma("SP", a.ident[:, :], cbf[:, 0:128])
    kb.dma("SP", a.colsel[:, :], cbf[:, 128:130])
    kb.dma("SP", a.selpair[:, :], cf32[0:2, 0:128])
    kb.dma("SP", a.ones65[64:65, :], cf32[64:65, 128:192])
    kb.dma("SP", a.pbf[:, :, :], pbf)
    kb.dma("SP", a.pf32[:, :, :], pf32)
    kb.memset("DVE", a.vbh[:, :, :], 0.0)
    kb.memset("DVE", a.vbl[:, :, :], 0.0)
    for h in range(g.H):
        kb.memset("DVE", a.vbh[:, :, 65 * h + 64:65 * h + 65], 1.0)


def emit_qk_pipe_s(kb, b, a, g, ps_in, ps_rq, ps_rot, T, sc_slot, q32):
    lv = b.lvec
    kb.act(b.sq32[:, 0:T], ps_in[:, 0:T], AF.Square)
    kb.cp("DVE", b.sqh[:, 0:T], b.sq32[:, 0:T])
    kb.tt("DVE", b.sql[:, 0:T], b.sq32[:, 0:T], b.sqh[:, 0:T], OP.subtract)
    kb.mm([(a.colsel[:, :], b.sqh[:, 0:T]),
           (a.colsel[:, :], b.sql[:, 0:T])], ps_rq[0:2, 0:T])
    kb.act(q32[:, 0:T], ps_in[:, 0:T], AF.Copy, scale=lv[:, sc_slot:sc_slot + 1])
    kb.mm([(a.pf32[:, 0, :], q32[:, 0:T])], ps_rot[:, 0:T])


def emit_qk_pipe_r(kb, b, a, g, ps_rq, ps_rot, T, is_k, q32, qp32, sdq):
    lv = b.lvec
    ct = a.rott[:, (2 if is_k else 0), 0:T]
    st = a.rott[:, (3 if is_k else 1), 0:T]
    kb.act(sdq[0:2, 0:T], ps_rq[0:2, 0:T], AF.Sqrt,
           bias=lv[0:2, SL_TINY:SL_TINY + 1])
    if is_k:
        kb.ts("DVE", sdq[0:2, 0:T], sdq[0:2, 0:T], 0.125,
              1.25e-13, OP.mult, OP.max)
    else:
        kb.ts("DVE", sdq[0:2, 0:T], sdq[0:2, 0:T], 1e-12, None,
              OP.max)
    kb.recip(sdq[0:2, 0:T], sdq[0:2, 0:T])
    kb.tt("DVE", a.t1[:, 0:T], q32[:, 0:T], ct, OP.mult)
    kb.tt("DVE", a.wv[:, 0:T], ps_rot[:, 0:T], st, OP.mult)
    kb.tt("DVE", qp32[:, 0:T], a.t1[:, 0:T], a.wv[:, 0:T], OP.add)


def emit_qk_pipe_b(kb, b, a, g, ps_bc, T, qp32, rqk):
    kb.mm([(a.selpair[:, :], rqk[0:2, 0:T])], ps_bc[:, 0:T])
    kb.tt("DVE", qp32[:, 0:T], qp32[:, 0:T], ps_bc[:, 0:T], OP.mult)
    kb.cp("ACT", a.qhb[:, 0:T], qp32[:, 0:T])
    kb.tt("DVE", a.qlb[:, 0:T], qp32[:, 0:T], a.qhb[:, 0:T], OP.subtract)


def emit_dup_forms(kb, a, ps_hi, ps_lo, T, hh, dst_hh, dst_ll):
    kb.mm([(a.pbf[:, hh, :], a.qhb[:, 0:T])], ps_hi[:, 0:T])
    kb.mm([(a.pbf[:, hh, :], a.qlb[:, 0:T])], ps_lo[:, 0:T])
    kb.cp("DVE", dst_hh, ps_hi[:, 0:T])
    kb.cp("DVE", dst_ll, ps_lo[:, 0:T])


def emit_stack_form(kb, a, ps_pk, T, hh, dst):
    kb.mm([(a.pbf[:, 2 + hh, :], a.qhb[:, 0:T]),
           (a.pbf[:, 4 + hh, :], a.qlb[:, 0:T])], ps_pk[:, 0:T])
    kb.cp("DVE", dst, ps_pk[:, 0:T])


def emit_attn_sublayer(kb, b, a, g, ps, li, wd, c0, c1, post_oc=None):
    T = c1 - c0
    NQ = T // 128
    gc0 = c0 // 128
    kb.dma("SP", a.rott[:, :, 0:T], wd["rott"][:, :, c0:c1])
    kb.dma("SP", a.msk[:, 0:T // 128, :, :], wd["mskin"][:, gc0:gc0 + T // 128, :, :])
    emit_build_t(kb, b, g, T, b.hsv, c_from=4)

    def thv(kc):
        return b.yh[:, kc, 0:T] if kc < 4 else b.th[:, kc, 0:T]

    def tlv(kc):
        return b.yl[:, kc, 0:T] if kc < 4 else b.tl[:, kc, 0:T]

    def qsel(hp):
        if hp % 2 == 0:
            return a.qhh, a.qll
        return b.gth, b.gtl

    if c0 == 0:
        kb.memset("DVE", a.khl[:, :, 0:256], 0.0)
        for tc in range(2):
            kb.memset("DVE", a.vbh[:, tc, :], 0.0)
            kb.memset("DVE", a.vbl[:, tc, :], 0.0)
            for h in range(g.H):
                kb.memset("DVE", a.vbh[:, tc, 65 * h + 64:65 * h + 65], 1.0)
    else:
        kb.cp("DVE", a.khl[:, :, 0:256], a.khl[:, :, T:T + 256])
        kb.cp("DVE", a.vbh[:, 0:2, :], a.vbh[:, T // 128:T // 128 + 2, :])
        kb.cp("DVE", a.vbl[:, 0:2, :], a.vbl[:, T // 128:T // 128 + 2, :])

    def attn_pair(hp):
        qh_, ql_ = qsel(hp)
        avb = {0: ps["H"], 1: ps["G"]}
        pend_av = None
        for hh in range(2):
            h = 2 * hp + hh
            for qs in range(NQ):
                it = hh * NQ + qs
                sc = [ps["D"], ps["E"]][it % 2]
                pb = it % 2
                for ci in range(3):
                    kcol = 128 * qs + 128 * ci
                    kb.mm([(a.khl[:, h, kcol:kcol + 128], qh_[:, hh, 128 * qs:128 * (qs + 1)]),
                           (a.khl[:, h, kcol:kcol + 128], ql_[:, hh, 128 * qs:128 * (qs + 1)]),
                           (a.ident[:, :], a.msk[:, qs, ci, :])],
                          sc[:, 128 * ci:128 * ci + 128])
                kb.act(a.p32[:, :], sc[:, 0:384], AF.Exp)
                kb.cp("ACT", a.ph[:, pb, :], a.p32[:, :])
                kb.tt("DVE", a.pl[:, pb, :], a.p32[:, :], a.ph[:, pb, :],
                      OP.subtract)
                if pend_av is not None:
                    kb.mm(*pend_av)
                    if it == NQ:
                        kb.ts("DVE", a.invd[64:65, 0, 0:T],
                              ps["H"][64:65, 0:T], 1e-30, None, OP.max)
                        kb.recip(a.invd[64:65, 0, 0:T], a.invd[64:65, 0, 0:T])
                vcol = 65 * h
                pav = []
                for ci in range(3):
                    vt = a.vbh[:, qs + ci, vcol:vcol + 65]
                    vtl = a.vbl[:, qs + ci, vcol:vcol + 65]
                    pch = a.ph[:, pb, 128 * ci:128 * ci + 128]
                    pcl = a.pl[:, pb, 128 * ci:128 * ci + 128]
                    pav += [(vt, pch), (vtl, pch), (vt, pcl)]
                pend_av = (pav, avb[hh][0:65, 128 * qs:128 * (qs + 1)])
        kb.mm(*pend_av)
        kb.ts("DVE", a.invd[64:65, 1, 0:T], ps["G"][64:65, 0:T], 1e-30,
              None, OP.max)
        kb.recip(a.invd[64:65, 1, 0:T], a.invd[64:65, 1, 0:T])
        kb.mm([(a.ones65[64:65, :], a.invd[64:65, 0, 0:T])],
              ps["F"][0:64, 0:T])
        kb.cp("ACT", a.invd[0:64, 0, 0:T], ps["F"][0:64, 0:T])
        kb.tt("DVE", a.an32[:, 0:T], ps["H"][0:64, 0:T],
              a.invd[0:64, 0, 0:T], OP.mult)
        kb.cp("ACT", a.aoh[0:64, hp, 0:T], a.an32[:, 0:T])
        kb.tt("DVE", a.aol[0:64, hp, 0:T], a.an32[:, 0:T],
              a.aoh[0:64, hp, 0:T], OP.subtract)

    def norm1(hp):
        kb.mm([(a.ones65[64:65, :], a.invd[64:65, 1, 0:T])],
              ps["D"][0:64, 0:T])
        kb.cp("ACT", a.invd[0:64, 1, 0:T], ps["D"][0:64, 0:T])
        kb.tt("DVE", a.an32[:, 0:T], ps["G"][0:64, 0:T],
              a.invd[0:64, 1, 0:T], OP.mult)
        kb.mm([(a.pf32[0:64, 2, :], a.an32[:, 0:T])], ps["D"][:, 0:T])
        kb.cp("ACT", a.aoh[64:128, hp, 0:T], ps["D"][64:128, 0:T])
        kb.tt("DVE", a.aol[64:128, hp, 0:T], ps["D"][64:128, 0:T],
              a.aoh[64:128, hp, 0:T], OP.subtract)

    for hp in range(9):
        if hp < 8:
            bi = hp % 2
            emit_w_load_pre(kb, b, wd, bi, 0, WS_QKV + 24 * hp, 24)
            pq = [(b.wh[bi][:, kc, :], thv(kc)) for kc in range(8)]
            pq += [(b.wh[bi][:, kc, :], tlv(kc)) for kc in range(8)]
            pq += [(b.wl[bi][:, kc, :], thv(kc)) for kc in range(8)]
            kb.mm(pq, ps["A"][:, 0:T])
            pk = [(b.wh[bi][:, 8 + kc, :], thv(kc)) for kc in range(8)]
            pk += [(b.wh[bi][:, 8 + kc, :], tlv(kc)) for kc in range(8)]
            pk += [(b.wl[bi][:, 8 + kc, :], thv(kc)) for kc in range(8)]
            kb.mm(pk, ps["B"][:, 0:T])
        if hp > 0:
            attn_pair(hp - 1)
        if hp == 8:
            norm1(7)
            break
        if getattr(a, "_evtok", None) is not None:
            kb.wait_token("PE", a._evtok)
        for tc in range(NQ):
            pv = [(thv(kc)[:, 128 * tc:128 * (tc + 1)], b.wh[bi][:, 16 + kc, :])
                  for kc in range(8)]
            pv += [(tlv(kc)[:, 128 * tc:128 * (tc + 1)], b.wh[bi][:, 16 + kc, :])
                   for kc in range(8)]
            pv += [(thv(kc)[:, 128 * tc:128 * (tc + 1)], b.wl[bi][:, 16 + kc, :])
                   for kc in range(8)]
            kb.mm(pv, ps["C"][:, 128 * tc:128 * tc + 128])
        if hp > 0:
            norm1(hp - 1)
        emit_qk_pipe_s(kb, b, a, g, ps["A"], ps["E"], ps["G"], T, SL_QS,
                       b.muR)
        emit_qk_pipe_s(kb, b, a, g, ps["B"], ps["F"], ps["H"], T, SL_KS,
                       b.ge32)
        emit_qk_pipe_r(kb, b, a, g, ps["E"], ps["G"], T, False,
                       b.muR, a.qp32, a.rqk)
        emit_qk_pipe_r(kb, b, a, g, ps["F"], ps["H"], T, True,
                       b.ge32, b.rR, a.invd[0:2, 0, :])
        qh_, ql_ = qsel(hp)
        emit_qk_pipe_b(kb, b, a, g, ps["G"], T, a.qp32, a.rqk)
        for hh in range(2):
            emit_dup_forms(kb, a, ps["D"], ps["F"], T, hh,
                           qh_[:, hh, 0:T], ql_[:, hh, 0:T])
        emit_qk_pipe_b(kb, b, a, g, ps["H"], T, b.rR, a.invd[0:2, 0, :])
        emit_stack_form(kb, a, ps["D"], T, 0, a.khl[:, 2 * hp, 256:256 + T])
        emit_stack_form(kb, a, ps["F"], T, 1, a.khl[:, 2 * hp + 1, 256:256 + T])
        for tc in range(NQ):
            for hh in range(2):
                h = 2 * hp + hh
                vcol = 65 * h
                kb.cp("ACT", a.vbh[:, 2 + tc, vcol:vcol + 64],
                      ps["C"][:, 128 * tc + 64 * hh:128 * tc + 64 * hh + 64])
                a._evtok = kb.tt(
                    "DVE", a.vbl[:, 2 + tc, vcol:vcol + 64],
                    ps["C"][:, 128 * tc + 64 * hh:128 * tc + 64 * hh + 64],
                    a.vbh[:, 2 + tc, vcol:vcol + 64], OP.subtract)
    for oc in range(g.FC):
        bi = oc % 2
        emit_w_load_pre(kb, b, wd, bi, 0, WS_WO + 8 * oc, 8)
        pX = [ps["A"], ps["B"], ps["C"], ps["G"]][oc % 4]
        pw = [(b.wh[bi][:, kc, :], a.aoh[:, kc, 0:T]) for kc in range(8)]
        pw += [(b.wh[bi][:, kc, :], a.aol[:, kc, 0:T]) for kc in range(8)]
        pw += [(b.wl[bi][:, kc, :], a.aoh[:, kc, 0:T]) for kc in range(8)]
        kb.mm(pw, pX[:, 0:T])
        kb.tt("DVE", b.r132[:, 0:T], pX[:, 0:T], b.yh[:, oc, 0:T],
              OP.add)
        kb.tt("DVE", b.y32[:, 0:T], b.r132[:, 0:T], b.yl[:, oc, 0:T],
              OP.add)
        kb.cp("ACT", b.yh[:, oc, 0:T], b.y32[:, 0:T])
        kb.tt("DVE", b.yl[:, oc, 0:T], b.y32[:, 0:T],
              b.yh[:, oc, 0:T], OP.subtract)
        if post_oc is not None:
            post_oc(oc)


def emit_inner(kb, b, a, g, ps, wdd, lvec_in, yhd, yld):
    for li in range(g.NL):
        wd = wdd(li)
        kb.dma("SP", b.lvec[:, :], lvec_in[li, :, :])
        emit_hsv_init(kb, b, g, b.hsv)
        emit_hsv_init(kb, b, g, b.hsv2)
        nst = g.st[li + 1] if li + 1 < g.NL else 0
        for ti, (c0, c1) in enumerate(g.tiles(li)):
            T = c1 - c0
            b.cm_col = wd["cmt"][:, ti:ti + 1] if "cmt" in wd else None
            for c in range(g.FC):
                kb.dma("SP", b.yh[:, c, 0:T], yhd[:, c, c0:c1])
                kb.dma("SP", b.yl[:, c, 0:T], yld[:, c, c0:c1])

            def ffn_pre(c, T=T):
                if c < 4:
                    kb.cp("ACT", b.th[:, c, 0:T], b.yh[:, c, 0:T])
                    kb.cp("DVE", b.tl[:, c, 0:T], b.yl[:, c, 0:T])
                else:
                    kb.cp("ACT", b.th[:, c, 0:1], b.hsv2[:, 0, c:c + 1])
                    kb.cp("DVE", b.tl[:, c, 0:1], b.hsv2[:, 1, c:c + 1])
                    kb.cp("ACT", b.th[:, c, 1:T], b.yh[:, c, 0:T - 1])
                    kb.cp("DVE", b.tl[:, c, 1:T], b.yl[:, c, 0:T - 1])
                    kb.cp("DVE", b.hsv2[:, 0, c:c + 1], b.yh[:, c, T - 1:T])
                    kb.cp("DVE", b.hsv2[:, 1, c:c + 1], b.yl[:, c, T - 1:T])
                kb.mm_acc([(b.ones128[:, :], b.th[:, c, 0:T]),
                           (b.ones128[:, :], b.tl[:, c, 0:T])],
                          ps["E"][:, 0:T], start=(c == 0),
                          stop=(c == g.FC - 1))
                kb.tt("DVE", b.t32[:, 0:T], b.th[:, c, 0:T],
                      b.tl[:, c, 0:T], OP.add)
                kb.act(b.sq32[:, 0:T], b.t32[:, 0:T], AF.Square)
                kb.cp("ACT", b.sqh[:, 0:T], b.sq32[:, 0:T])
                kb.tt("DVE", b.sql[:, 0:T], b.sq32[:, 0:T], b.sqh[:, 0:T],
                      OP.subtract)
                kb.mm_acc([(b.ones128[:, :], b.sqh[:, 0:T]),
                           (b.ones128[:, :], b.sql[:, 0:T])],
                          ps["F"][:, 0:T], start=(c == 0),
                          stop=(c == g.FC - 1))
            emit_attn_sublayer(kb, b, a, g, ps, li, wd, c0, c1,
                               post_oc=(ffn_pre if STATS_FOLD else None))
            emit_ffn_sublayer(kb, b, g, ps, li, wd, c0, c1,
                              pre_done=STATS_FOLD)
            b.cm_col = None
            if c0 <= nst - 1 < c1:
                j = nst - 1 - c0
                for c in range(g.FC):
                    kb.cp("DVE", b.hsnap[:, 0, c:c + 1], b.yh[:, c, j:j + 1])
                    kb.cp("DVE", b.hsnap[:, 1, c:c + 1], b.yl[:, c, j:j + 1])
            kb.dma("SP", yhd[:, :, c0:c1], b.yh[:, :, 0:T])
            kb.dma("SP", yld[:, :, c0:c1], b.yl[:, :, 0:T])


ALIBI_CUT = 112.0

def alibi_slopes():
    return np.array([2.0 ** (-8.0 / n) for n in range(16, 0, -1)], np.float64)


class OPlan:

    def __init__(self, g, S=4096, baseL=-512, baseR=1024):
        self.g = g
        self.S = S
        self.NKC = S // 128
        self.NQT = g.W // 512
        self.bL = baseL // 128
        self.bR = baseR // 128
        self.baseL = baseL
        self.baseR = baseR
        sl = alibi_slopes()
        self.slopes = sl
        self.cut = [ALIBI_CUT / s for s in sl]
        self.entries = []
        self.NE = 0
        self.ebidx = {}
        for h in range(16):
            eh = []
            for qt in range(self.NQT):
                lst = []
                for kc in range(self.NKC):
                    d = kc - 4 * qt
                    if self.bL <= d <= self.bL + 3:
                        cls, z = 2, d - self.bL
                    elif self.bR <= d <= self.bR + 3:
                        cls, z = 4, d - self.bR
                    elif d < self.bL:
                        cls, z = 0, 0
                    elif d < self.bR:
                        cls, z = 3, 0
                    else:
                        cls, z = 1, 0
                    keep = False
                    for base in (self.baseL, self.baseR):
                        i0 = base + 512 * qt
                        dist = max(i0 - (128 * kc + 127), 128 * kc - (i0 + 511), 0)
                        if dist <= ALIBI_CUT / sl[h]:
                            keep = True
                    if keep:
                        if cls in (0, 1, 3):
                            self.ebidx[(h, qt, kc)] = self.NE
                            self.NE += 1
                        lst.append((kc, cls, z))
                eh.append(lst)
            self.entries.append(eh)


def build_outer_data(plan, base):
    sl = plan.slopes
    NE = plan.NE
    eb = np.zeros((128, max(NE, 1)), np.float32)
    mz = np.zeros((16, 2, 4, 128, 512), np.float32)
    gt = np.zeros((16, 5, 65, 512), np.float32)
    jr = np.arange(128, dtype=np.float64)
    av = np.arange(512, dtype=np.float64)
    C2 = 576.0
    own = {2: base == plan.baseL, 4: base == plan.baseR}
    for h in range(16):
        s = sl[h]
        g_past = np.exp(-s * (av + 64.0))
        g_fut = np.exp(-s * (C2 - av))
        ones = np.ones(512)
        grow = {0: g_past, 1: g_fut}
        grow[3] = g_fut if base == plan.baseL else g_past
        grow[2] = ones if own[2] else g_past
        grow[4] = ones if own[4] else g_fut
        for cls in range(5):
            gt[h, cls, 0:65, :] = grow[cls][None, :]
        for zi, cls in ((0, 2), (1, 4)):
            for z in range(4):
                if own[cls]:
                    dj = (128 * z + jr)[:, None] - av[None, :]
                    mz[h, zi, z] = np.exp(-s * np.abs(dj)).astype(np.float32)
                else:
                    if cls == 2:
                        off = 128.0 * (z + plan.bL - plan.bR) + jr
                        mz[h, zi, z] = np.exp(s * (off + 64.0))[:, None].astype(np.float32)
                    else:
                        off = 128.0 * (z + plan.bR - plan.bL) + jr
                        mz[h, zi, z] = np.exp(s * (C2 - off))[:, None].astype(np.float32)
        for qt in range(plan.NQT):
            i0 = base + 512 * qt
            for (kc, cls, z) in plan.entries[h][qt]:
                if cls in (2, 4):
                    continue
                e = plan.ebidx[(h, qt, kc)]
                j = 128.0 * kc + jr
                if (cls == 0) or (cls == 3 and base == plan.baseR):
                    eb[:, e] = np.exp(s * (j - i0 + 64.0))
                else:
                    eb[:, e] = np.exp(s * (i0 + C2 - j))
    return eb, mz, gt


def alloc_outer_bufs(kb, g, plan, gl=None):
    o = Bufs()
    W = g.W
    o.xth = kb.sb("oxth", [128, 8, 512], BF16)
    o.xtl = kb.sb("oxtl", [128, 8, 512], BF16)
    o.qhh = kb.sb("oqhh", [128, 2, W], BF16)
    o.qll = kb.sb("oqll", [128, 2, W], BF16)
    o.khl = kb.sb("okhl", [128, 2, plan.S], BF16)
    o.vbh = kb.sb("ovbh", [128, plan.S // 128, 130], BF16)
    o.vbl = kb.sb("ovbl", [128, plan.S // 128, 130], BF16)
    o.aoph = kb.sb("oaoph", [128, W], BF16)
    o.aopl = kb.sb("oaopl", [128, W], BF16)
    o.p32 = gl.sq32 if gl is not None else kb.sb("op32", [128, 512], F32)
    o.ph = kb.sb("oph", [128, 512], BF16)
    o.pl = kb.sb("opl", [128, 512], BF16)
    o.u32 = gl.u32 if gl is not None else kb.sb("ou32", [128, 512], F32)
    o.mz = (gl.x32 if gl is not None else
            kb.sb("omz", [128, 8, 512], F32))
    o.gt = kb.sb("ogt", [65, 5, 512], F32)
    o.ebt = kb.sb("oebt", [128, max(plan.NE, 1)], F32)
    o.att32 = kb.sb("oatt32", [65, 512], F32)
    o.invd = kb.sb("oinvd", [65, 512], F32)
    o.invdR = kb.sb("oinvdR", [64, 512], F32)
    o.ones65 = kb.sb("oones65", [65, 64], F32)
    o.pbf = kb.sb("opbf", [128, 6, 128], BF16)
    o.pf32 = kb.sb("opf32", [128, 4, 128], F32)
    o.qhb = kb.sb("oqhb", [128, 512], BF16)
    o.qlb = kb.sb("oqlb", [128, 512], BF16)
    o.an1 = kb.sb("oan1", [64, 512], F32)
    o.xth2 = gl.aoh if gl is not None else kb.sb("oxth2", [128, 8, 512], BF16)
    o.xtl2 = gl.aol if gl is not None else kb.sb("oxtl2", [128, 8, 512], BF16)
    o.qhb2 = gl.oh if gl is not None else kb.sb("oqhb2", [128, 512], BF16)
    o.qlb2 = gl.ol if gl is not None else kb.sb("oqlb2", [128, 512], BF16)
    o.e32b = gl.muR if gl is not None else kb.sb("oe32b", [128, 512], F32)
    o.u32b = gl.m2 if gl is not None else kb.sb("ou32b", [128, 512], F32)
    o.ph2 = gl.sqh if gl is not None else kb.sb("oph2", [128, 512], BF16)
    o.pl2 = gl.sql if gl is not None else kb.sb("opl2", [128, 512], BF16)
    return o


def emit_outer_attn(kb, b, o, g, plan, pso, wq, wk, wv, xfh, xfl, xsh, xsl,
                    mzin, gtin, ebin, aodh, aodl, cf32o=None, pbfo=None,
                    pf32o=None, dbg=None, bg=None):
    if bg is None:
        bg = lambda n=1: None
    """Outer ALiBi attention. x' full (keys) in xfh/xfl [128,8,S] bf16 HBM;
    x' span (queries) in xsh/xsl [128,8,W]. Writes per-pair attention output
    (normalized, hi/lo) to aodh/aodl [8, 128, W] HBM scratch."""
    W = g.W
    NQT = plan.NQT
    NKT = plan.S // 512
    kb.dma("SP", o.ebt[:, :], ebin)
    kb.dma("SP", o.ones65[64:65, :], cf32o[64:65, 128:192])
    kb.dma("SP", o.pbf[:, :, :], pbfo)
    kb.dma("SP", o.pf32[:, :, :], pf32o)
    kb.memset("DVE", o.vbh[:, :, :], 0.0)
    kb.memset("DVE", o.vbl[:, :, :], 0.0)
    for hh in range(2):
        kb.memset("DVE", o.vbh[:, :, 65 * hh + 64:65 * hh + 65], 1.0)
    for hp in range(8):
        bi = hp % 2
        emit_w_load(kb, b, g, wq, 128 * hp, 128, bi, slot0=0)
        emit_w_load(kb, b, g, wk, 128 * hp, 128, bi, slot0=8)
        emit_w_load(kb, b, g, wv, 128 * hp, 128, bi, slot0=16)
        xb = [(o.xth, o.xtl), (o.xth2, o.xtl2)]
        qb = [(o.qhb, o.qlb), (o.qhb2, o.qlb2)]
        for qt in range(NQT):
            xh_, xl_ = xb[qt % 2]
            qh_, ql_ = qb[qt % 2]
            sc_ = pso["SC"][:, 512 * (qt % 2):512 * (qt % 2) + 512]
            kb.dma("SP", xh_[:, :, :], xsh[:, :, 512 * qt:512 * (qt + 1)])
            kb.dma("POOL", xl_[:, :, :], xsl[:, :, 512 * qt:512 * (qt + 1)])
            pq = [(b.wh[bi][:, kc, :], xh_[:, kc, :]) for kc in range(8)]
            pq += [(b.wh[bi][:, kc, :], xl_[:, kc, :]) for kc in range(8)]
            pq += [(b.wl[bi][:, kc, :], xh_[:, kc, :]) for kc in range(8)]
            kb.mm(pq, sc_)
            sl_ = slice(512 * qt, 512 * (qt + 1))
            kb.cp("DVE", qh_[:, :], sc_)
            kb.tt("DVE", ql_[:, :], sc_, qh_[:, :], OP.subtract)
            for hh in range(2):
                kb.mm([(o.pbf[:, hh, :], qh_[:, :])], pso["BC"][:, 0:512])
                kb.mm([(o.pbf[:, hh, :], ql_[:, :])], pso["V1"][:, 0:512])
                kb.cp("DVE", o.qhh[:, hh, sl_], pso["BC"][:, 0:512])
                kb.cp("DVE", o.qll[:, hh, sl_], pso["V1"][:, 0:512])
            bg(4)
        for kt in range(NKT):
            xh_, xl_ = xb[kt % 2]
            qh_, ql_ = qb[kt % 2]
            sc_ = pso["SC"][:, 512 * (kt % 2):512 * (kt % 2) + 512]
            u32_ = [o.u32, o.u32b][kt % 2]
            vb_ = pso["V0"] if kt % 2 == 0 else pso["V2"]
            kb.dma("SP", xh_[:, :, :], xfh[:, :, 512 * kt:512 * (kt + 1)])
            kb.dma("POOL", xl_[:, :, :], xfl[:, :, 512 * kt:512 * (kt + 1)])
            pk = [(b.wh[bi][:, 8 + kc, :], xh_[:, kc, :]) for kc in range(8)]
            pk += [(b.wh[bi][:, 8 + kc, :], xl_[:, kc, :]) for kc in range(8)]
            pk += [(b.wl[bi][:, 8 + kc, :], xh_[:, kc, :]) for kc in range(8)]
            kb.mm(pk, sc_)
            kb.act(u32_[:, 0:512], sc_, AF.Copy, scale=0.125)
            sl_ = slice(512 * kt, 512 * (kt + 1))
            kb.cp("DVE", qh_[:, :], u32_[:, 0:512])
            kb.tt("DVE", ql_[:, :], u32_[:, 0:512], qh_[:, :], OP.subtract)
            kb.mm([(o.pbf[:, 2, :], qh_[:, :]),
                   (o.pbf[:, 4, :], ql_[:, :])], pso["BC"][:, 0:512])
            kb.mm([(o.pbf[:, 3, :], qh_[:, :]),
                   (o.pbf[:, 5, :], ql_[:, :])], pso["V1"][:, 0:512])
            kb.cp("DVE", o.khl[:, 0, sl_], pso["BC"][:, 0:512])
            kb.cp("DVE", o.khl[:, 1, sl_], pso["V1"][:, 0:512])
            if getattr(o, "_evtok", None) is not None:
                kb.wait_token("PE", o._evtok)
            tkv = None
            for tc in range(4):
                pv = [(xh_[:, kc, 128 * tc:128 * (tc + 1)], b.wh[bi][:, 16 + kc, :])
                      for kc in range(8)]
                pv += [(xl_[:, kc, 128 * tc:128 * (tc + 1)], b.wh[bi][:, 16 + kc, :])
                       for kc in range(8)]
                pv += [(xh_[:, kc, 128 * tc:128 * (tc + 1)], b.wl[bi][:, 16 + kc, :])
                       for kc in range(8)]
                tkv = kb.mm(pv, vb_[:, 128 * tc:128 * tc + 128])
            kb.wait_token("DVE", tkv)
            for tc in range(4):
                for hh in range(2):
                    vc = 65 * hh
                    kb.cp("DVE", o.vbh[:, 4 * kt + tc, vc:vc + 64],
                          vb_[:, 128 * tc + 64 * hh:128 * tc + 64 * hh + 64])
                    o._evtok = kb.tt(
                        "DVE", o.vbl[:, 4 * kt + tc, vc:vc + 64],
                        vb_[:, 128 * tc + 64 * hh:128 * tc + 64 * hh + 64],
                        o.vbh[:, 4 * kt + tc, vc:vc + 64], OP.subtract)
            bg(4)
        for hh in range(2):
            h = 2 * hp + hh
            kb.dma("POOL", o.mz[:, :, :], mzin[h, :, :, :, :].rearrange("a b p q -> p (a b) q"))
            kb.dma("ACT", o.gt[:, :, :], gtin[h, :, :, :].rearrange("c p q -> p c q"))
            for qt in range(NQT):
                ents = plan.entries[h][qt]
                if not ents:
                    continue
                cls_last = {}
                cls_first = {}
                for (kc, cls, z) in ents:
                    cls_last[cls] = kc
                    if cls not in cls_first:
                        cls_first[cls] = kc
                qsl_ = slice(512 * qt, 512 * (qt + 1))
                pend = None
                for i, (kc, cls, z) in enumerate(ents):
                    half = i % 2
                    ph_, pl_ = (o.ph, o.pl) if half == 0 else (o.ph2, o.pl2)
                    e32 = [o.p32, o.e32b][half][:, 0:512]
                    u32_ = [o.u32, o.u32b][half]
                    kb.mm([(o.khl[:, hh, 128 * kc:128 * (kc + 1)], o.qhh[:, hh, qsl_]),
                           (o.khl[:, hh, 128 * kc:128 * (kc + 1)], o.qll[:, hh, qsl_])],
                          pso["SC"][:, 512 * half:512 * (half + 1)])
                    kb.act(e32, pso["SC"][:, 512 * half:512 * (half + 1)],
                           AF.Exp)
                    if cls in (2, 4):
                        zi = 0 if cls == 2 else 4
                        kb.tt("DVE", u32_[:, :], e32, o.mz[:, zi + z, :], OP.mult)
                        kb.cp("DVE", ph_[:, :], u32_[:, :])
                        kb.tt("DVE", pl_[:, :], u32_[:, :], ph_[:, :], OP.subtract)
                    else:
                        e = plan.ebidx[(h, qt, kc)]
                        kb.ts("DVE", ph_[:, :], e32, o.ebt[:, e:e + 1], None, OP.mult)
                        kb.stt(pl_[:, :], e32, o.ebt[:, e:e + 1], ph_[:, :],
                               OP.mult, OP.subtract)
                    if pend is not None:
                        kb.mm_acc(*pend[0], **pend[1])
                    bg(1)
                    vc = 65 * hh
                    vt_h = o.vbh[:, kc, vc:vc + 65]
                    vt_l = o.vbl[:, kc, vc:vc + 65]
                    pend = (([(vt_h, ph_[:, :]), (vt_l, ph_[:, :]),
                              (vt_h, pl_[:, :])],
                             pso["V%d" % cls][0:65, 0:512]),
                            dict(start=(cls_first[cls] == kc),
                                 stop=(cls_last[cls] == kc)))
                if pend is not None:
                    kb.mm_acc(*pend[0], **pend[1])
                pres = sorted(cls_first.keys())
                first = True
                for cls in pres:
                    if first:
                        kb.tt("DVE", o.att32[:, :], pso["V%d" % cls][0:65, 0:512],
                              o.gt[:, cls, :], OP.mult)
                        first = False
                    else:
                        kb.tt("DVE", o.u32[0:65, :], pso["V%d" % cls][0:65, 0:512],
                              o.gt[:, cls, :], OP.mult)
                        kb.tt("DVE", o.att32[:, :], o.att32[:, :], o.u32[0:65, :], OP.add)
                if dbg is not None:
                    kb.dma("SP", dbg[h, qt, :, :], o.att32[:, :])
                kb.ts("DVE", o.invd[64:65, :], o.att32[64:65, :], 1e-30, None,
                      OP.max)
                kb.recip(o.invd[64:65, :], o.invd[64:65, :])
                kb.mm([(o.ones65[64:65, :], o.invd[64:65, :])],
                      pso["BC"][0:64, 0:512])
                kb.cp("ACT", o.invdR[:, :], pso["BC"][0:64, 0:512])
                kb.tt("DVE", o.an1[:, :], o.att32[0:64, :], o.invdR[:, :],
                      OP.mult)
                if hh == 0:
                    kb.cp("DVE", o.aoph[0:64, qsl_], o.an1[:, :])
                    kb.tt("DVE", o.aopl[0:64, qsl_], o.an1[:, :],
                          o.aoph[0:64, qsl_], OP.subtract)
                else:
                    kb.mm([(o.pf32[0:64, 2, :], o.an1[:, :])],
                          pso["BC"][:, 0:512])
                    kb.cp("DVE", o.aoph[64:128, qsl_], pso["BC"][64:128, 0:512])
                    kb.tt("DVE", o.aopl[64:128, qsl_], pso["BC"][64:128, 0:512],
                          o.aoph[64:128, qsl_], OP.subtract)
                bg(2)
        kb.dma("ACT", aodh[hp, :, :], o.aoph[:, :])
        kb.dma("ACT", aodl[hp, :, :], o.aopl[:, :])


OV_G = 0
OV_B = 8
OV_EPS = 16


def alloc_glue_bufs(kb, g):
    gl = Bufs()
    gl.x32 = kb.sb("glx32", [128, 8, 512], F32)
    gl.sq32 = kb.sb("glsq", [128, 512], F32)
    gl.sqh = kb.sb("glsqh", [128, 512], BF16)
    gl.sql = kb.sb("glsql", [128, 512], BF16)
    gl.muR = kb.sb("glmu", [128, 512], F32)
    gl.m2 = kb.sb("glm2", [128, 512], F32)
    gl.rR = kb.sb("glrR", [128, 512], F32)
    gl.u32 = kb.sb("glu32", [128, 512], F32)
    gl.v32 = kb.sb("glv32", [128, 512], F32)
    gl.oh = kb.sb("gloh", [128, 512], BF16)
    gl.ol = kb.sb("glol", [128, 512], BF16)
    gl.ones128 = kb.sb("glones", [128, 128], BF16)
    gl.ovec = kb.sb("glovec", [128, 32], F32)
    gl.aoh = kb.sb("glaoh", [128, 8, 512], BF16)
    gl.aol = kb.sb("glaol", [128, 8, 512], BF16)
    return gl


def _glue_stats(kb, gl, ps, T, lnln):
    passes = []
    for c in range(8):
        kb.cp("DVE", gl.sqh[:, 0:T], gl.x32[:, c, 0:T])
        kb.tt("DVE", gl.sql[:, 0:T], gl.x32[:, c, 0:T], gl.sqh[:, 0:T], OP.subtract)
        kb.mm_acc([(gl.ones128[:, :], gl.sqh[:, 0:T]),
                   (gl.ones128[:, :], gl.sql[:, 0:T])],
                  ps["E"][:, 0:T], start=(c == 0), stop=(c == 7))
        kb.act(gl.sq32[:, 0:T], gl.x32[:, c, 0:T], AF.Square)
        kb.cp("DVE", gl.sqh[:, 0:T], gl.sq32[:, 0:T])
        kb.tt("DVE", gl.sql[:, 0:T], gl.sq32[:, 0:T], gl.sqh[:, 0:T], OP.subtract)
        kb.mm_acc([(gl.ones128[:, :], gl.sqh[:, 0:T]),
                   (gl.ones128[:, :], gl.sql[:, 0:T])],
                  ps["F"][:, 0:T], start=(c == 0), stop=(c == 7))
    kb.ts("DVE", gl.muR[:, 0:T], ps["E"][:, 0:T], INV_D, None, OP.mult)
    kb.tt("DVE", gl.m2[:, 0:T], gl.muR[:, 0:T], gl.muR[:, 0:T], OP.mult)
    kb.stt(gl.u32[:, 0:T], ps["F"][:, 0:T], INV_D, gl.m2[:, 0:T],
           OP.mult, OP.subtract)
    if not lnln:
        kb.act(gl.m2[:, 0:T], gl.u32[:, 0:T], AF.Sqrt,
               bias=gl.ovec[:, OV_EPS:OV_EPS + 1])
        kb.recip(gl.rR[:, 0:T], gl.m2[:, 0:T])
    else:
        kb.ts("DVE", gl.m2[:, 0:T], gl.u32[:, 0:T], 1e-5, None, OP.add)
        kb.recip(gl.v32[:, 0:T], gl.m2[:, 0:T])
        kb.stt(gl.v32[:, 0:T], gl.u32[:, 0:T], 1.0, gl.v32[:, 0:T],
               OP.mult, OP.mult)
        kb.ts("DVE", gl.v32[:, 0:T], gl.v32[:, 0:T], 1e-5, None, OP.add)
        kb.tt("DVE", gl.m2[:, 0:T], gl.m2[:, 0:T], gl.v32[:, 0:T], OP.mult)
        kb.act(gl.m2[:, 0:T], gl.m2[:, 0:T], AF.Sqrt)
        kb.recip(gl.rR[:, 0:T], gl.m2[:, 0:T])


def emit_xprime(kb, gl, g, ps, xin, N, outh, outl, affine):
    nt = N // 512
    for t in range(nt):
        sl_ = slice(512 * t, 512 * (t + 1))
        kb.dma("SP", gl.x32[:, :, :], xin[:, :, sl_])
        _glue_stats(kb, gl, ps, 512, lnln=True)
        for c in range(8):
            kb.tt("DVE", gl.u32[:, :], gl.x32[:, c, :], gl.muR[:, :], OP.subtract)
            kb.tt("DVE", gl.u32[:, :], gl.u32[:, :], gl.rR[:, :], OP.mult)
            if affine:
                kb.ts("DVE", gl.u32[:, :], gl.u32[:, :],
                      gl.ovec[:, OV_G + c:OV_G + c + 1],
                      gl.ovec[:, OV_B + c:OV_B + c + 1], OP.mult, OP.add)
            kb.cp("DVE", gl.oh[:, :], gl.u32[:, :])
            kb.tt("DVE", gl.ol[:, :], gl.u32[:, :], gl.oh[:, :], OP.subtract)
            kb.dma("SP", outh[:, c, sl_], gl.oh[:, :])
            kb.dma("SP", outl[:, c, sl_], gl.ol[:, :])


def emit_wo_x2_y0(kb, bw, gl, g, ps, wo_dram, aodh, aodl, xcore, x2d, yhd, yld):
    nt = g.W // 512
    ao_h = gl.aoh
    ao_l = gl.aol
    for t in range(nt):
        sl_ = slice(512 * t, 512 * (t + 1))
        kb.dma("SP", ao_h[:, :, :], aodh[:, :, sl_].rearrange("a p q -> p a q"))
        kb.dma("SP", ao_l[:, :, :], aodl[:, :, sl_].rearrange("a p q -> p a q"))
        for oc in range(8):
            bi = oc % 2
            emit_w_load(kb, bw, g, wo_dram, 128 * oc, 128, bi, slot0=0)
            pw = [(bw.wh[bi][:, kc, :], ao_h[:, kc, :]) for kc in range(8)]
            pw += [(bw.wh[bi][:, kc, :], ao_l[:, kc, :]) for kc in range(8)]
            pw += [(bw.wl[bi][:, kc, :], ao_h[:, kc, :]) for kc in range(8)]
            kb.mm(pw, ps["A"][:, 0:512])
            kb.cp("ACT", gl.x32[:, oc, :], ps["A"][:, 0:512])
        _glue_stats(kb, gl, ps, 512, lnln=False)
        for c in range(8):
            kb.tt("DVE", gl.u32[:, :], gl.x32[:, c, :], gl.muR[:, :], OP.subtract)
            kb.tt("DVE", gl.u32[:, :], gl.u32[:, :], gl.rR[:, :], OP.mult)
            kb.dma("SP", gl.v32[:, :], xcore[:, c, sl_])
            kb.tt("DVE", gl.x32[:, c, :], gl.u32[:, :], gl.v32[:, :], OP.add)
            kb.dma("SP", x2d[:, c, sl_], gl.x32[:, c, :])
        _glue_stats(kb, gl, ps, 512, lnln=False)
        for c in range(8):
            kb.tt("DVE", gl.u32[:, :], gl.x32[:, c, :], gl.muR[:, :], OP.subtract)
            kb.tt("DVE", gl.u32[:, :], gl.u32[:, :], gl.rR[:, :], OP.mult)
            kb.cp("DVE", gl.oh[:, :], gl.u32[:, :])
            kb.tt("DVE", gl.ol[:, :], gl.u32[:, :], gl.oh[:, :], OP.subtract)
            kb.dma("SP", yhd[:, c, sl_], gl.oh[:, :])
            kb.dma("SP", yld[:, c, sl_], gl.ol[:, :])


def emit_final(kb, gl, g, ps, yhd, yld, x2d, outd):
    nt = g.W // 512
    for t in range(nt):
        sl_ = slice(512 * t, 512 * (t + 1))
        for c in range(8):
            kb.dma("SP", gl.oh[:, :], yhd[:, c, sl_])
            kb.dma("SP", gl.ol[:, :], yld[:, c, sl_])
            kb.tt("DVE", gl.x32[:, c, :], gl.oh[:, :], gl.ol[:, :], OP.add)
        _glue_stats(kb, gl, ps, 512, lnln=False)
        for c in range(8):
            kb.tt("DVE", gl.u32[:, :], gl.x32[:, c, :], gl.muR[:, :], OP.subtract)
            kb.tt("DVE", gl.u32[:, :], gl.u32[:, :], gl.rR[:, :], OP.mult)
            kb.dma("SP", gl.v32[:, :], x2d[:, c, sl_])
            kb.tt("DVE", gl.u32[:, :], gl.u32[:, :], gl.v32[:, :], OP.add)
            kb.dma("SP", outd[:, c, sl_], gl.u32[:, :])


def build_full(g, plan, NL):
    S = plan.S
    W = g.W
    nc = bass.Bass("TRN2", target_bir_lowering=False)
    ap = lambda t: t.ap()
    xfull = nc.dram_tensor("xfull", [128, 8, S], F32, kind="ExternalInput").ap()
    xcore = nc.dram_tensor("xcore", [128, 8, W], F32, kind="ExternalInput").ap()
    owq = nc.dram_tensor("owq", [1024, 1024], F32, kind="ExternalInput").ap()
    owk = nc.dram_tensor("owk", [1024, 1024], F32, kind="ExternalInput").ap()
    owv = nc.dram_tensor("owv", [1024, 1024], F32, kind="ExternalInput").ap()
    owo = nc.dram_tensor("owo", [1024, 1024], F32, kind="ExternalInput").ap()
    ovin = nc.dram_tensor("ovin", [128, 32], F32, kind="ExternalInput").ap()
    fqkv = nc.dram_tensor("fqkv", [NL, 1024, 3072], F32, kind="ExternalInput").ap()
    fwo = nc.dram_tensor("fwo", [NL, 1024, 1024], F32, kind="ExternalInput").ap()
    fw1 = nc.dram_tensor("fw1", [NL, 1024, 5460], F32, kind="ExternalInput").ap()
    fw2 = nc.dram_tensor("fw2", [NL, 2730, 1024], F32, kind="ExternalInput").ap()
    lvin = nc.dram_tensor("lvin", [NL, 128, SL_NV], F32, kind="ExternalInput").ap()
    rott = nc.dram_tensor("rott_in", [128, 4, W], F32, kind="ExternalInput").ap()
    mskin = nc.dram_tensor("msk_in", [128, W // 128, 3, 128], BF16, kind="ExternalInput").ap()
    cbf = nc.dram_tensor("cbf_in", [128, 130], BF16, kind="ExternalInput").ap()
    cf32 = nc.dram_tensor("cf32_in", [128, 192], F32, kind="ExternalInput").ap()
    pbfi = nc.dram_tensor("pbf_in", [128, 6, 128], BF16, kind="ExternalInput").ap()
    pf32i = nc.dram_tensor("pf32_in", [128, 4, 128], F32, kind="ExternalInput").ap()
    mzin = nc.dram_tensor("mzin", [16, 2, 4, 128, 512], F32, kind="ExternalInput").ap()
    gtin = nc.dram_tensor("gtin", [16, 5, 65, 512], F32, kind="ExternalInput").ap()
    ebin = nc.dram_tensor("ebin", [128, max(plan.NE, 1)], F32, kind="ExternalInput").ap()
    outd = nc.dram_tensor("outd", [128, 8, W], F32, kind="ExternalOutput").ap()
    xph = nc.dram_tensor("xph", [128, 8, S], BF16).ap()
    xpl = nc.dram_tensor("xpl", [128, 8, S], BF16).ap()
    xsh = nc.dram_tensor("xsh", [128, 8, W], BF16).ap()
    xsl = nc.dram_tensor("xsl", [128, 8, W], BF16).ap()
    aodh = nc.dram_tensor("aodh", [8, 128, W], BF16).ap()
    aodl = nc.dram_tensor("aodl", [8, 128, W], BF16).ap()
    x2d = nc.dram_tensor("x2d", [128, 8, W], F32).ap()
    yhd = nc.dram_tensor("yhd", [128, 8, W], BF16).ap()
    yld = nc.dram_tensor("yld", [128, 8, W], BF16).ap()
    whd = nc.dram_tensor("whd", [NL, 128, WS_N, 128], BF16).ap()
    wld = nc.dram_tensor("wld", [NL, 128, WS_N, 128], BF16).ap()

    with contextlib.ExitStack() as gstk:
        kb = KBI(nc, gstk)
        bw = alloc_wbufs(kb)
        with contextlib.ExitStack() as stk2:
            kb.st = stk2
            gl = alloc_glue_bufs(kb, g)
            with contextlib.ExitStack() as stkx:
                kb.st = stkx
                pstmp = {"E": kb.ps("psxE", [128, 512], F32),
                         "F": kb.ps("psxF", [128, 512], F32)}
                with kb.block("xprime"):
                    kb.dma("SP", gl.ovec[:, :], ovin)
                    kb.memset("POOL", gl.ones128[:, :], 1.0)
                    emit_xprime(kb, gl, g, pstmp, xfull, S, xph, xpl, True)
                    emit_xprime(kb, gl, g, pstmp, xcore, W, xsh, xsl, True)
                    kb.drain_dmas()
                kb.st = stk2
            with contextlib.ExitStack() as stk3:
                kb.st = stk3
                o = alloc_outer_bufs(kb, g, plan, gl=gl)
                sp = alloc_wsplit_bufs(kb)
                thunks = []
                for li in range(NL):
                    thunks += ws_layer_thunks(kb, sp, g, fqkv[li], fwo[li],
                                              fw1[li], fw2[li], whd[li],
                                              wld[li])
                tpos = [0]

                def bg(n=1):
                    j = tpos[0]
                    e = min(j + n, len(thunks))
                    for k2 in range(j, e):
                        thunks[k2]()
                    tpos[0] = e
                pso = {"SC": kb.ps("psSC", [128, 1024], F32),
                       "BC": kb.ps("psBC", [128, 512], F32)}
                for i in range(5):
                    pso["V%d" % i] = kb.ps("psV%d" % i, [128, 512], F32)
                with kb.block("oattn"):
                    kb.memset("DVE", sp.sh[:, :, :, :], 0.0)
                    kb.memset("DVE", sp.sl[:, :, :, :], 0.0)
                    emit_outer_attn(kb, bw, o, g, plan, pso, owq, owk, owv,
                                    xph, xpl, xsh, xsl, mzin, gtin, ebin,
                                    aodh, aodl, cf32o=cf32, pbfo=pbfi,
                                    pf32o=pf32i, bg=bg)
                    bg(len(thunks))
                    kb.drain_dmas()
                kb.st = stk2
            pstmp2 = {"A": kb.ps("pswA", [128, 512], F32),
                      "E": kb.ps("pswE", [128, 512], F32),
                      "F": kb.ps("pswF", [128, 512], F32)}
            with kb.block("wo"):
                emit_wo_x2_y0(kb, bw, gl, g, pstmp2, owo, aodh, aodl, xcore,
                              x2d, yhd, yld)
                kb.drain_dmas()
            kb.st = gstk
        with contextlib.ExitStack() as stk4:
            kb.st = stk4
            b = alloc_inner_bufs(kb, g, bw)
            a = alloc_attn_bufs(kb, g)
            ps = {k: kb.ps(f"ps{k}", [128, 512], F32) for k in
                  ["A", "B", "C", "D", "E", "F", "G", "H"]}
            with kb.block("iinit"):
                kb.memset("POOL", b.ones128[:, :], 1.0)
                kb.memset("DVE", b.hsnap[:, :, :], 0.0)
                emit_attn_consts(kb, b, a, g, cbf, cf32, pbfi, pf32i)
            with kb.block("inner"):
                def wdd(li):
                    return {"whd": whd[li], "wld": wld[li],
                            "rott": rott, "mskin": mskin}
                emit_inner(kb, b, a, g, ps, wdd, lvin, yhd, yld)
                kb.drain_dmas()
            kb.st = gstk
        with contextlib.ExitStack() as stk5:
            kb.st = stk5
            gl2 = alloc_glue_bufs(kb, g)
            psf = {"E": kb.ps("psfE", [128, 512], F32),
                   "F": kb.ps("psfF", [128, 512], F32)}
            with kb.block("final"):
                kb.dma("SP", gl2.ovec[:, :], ovin)
                kb.memset("POOL", gl2.ones128[:, :], 1.0)
                emit_final(kb, gl2, g, psf, yhd, yld, x2d, outd)
                kb.drain_dmas()
            kb.st = gstk
    return nc


def host_masks(base, ngc):
    msk = np.zeros((128, ngc, 3, 128), BF)
    NEG = np.float32(-1e38)
    kr = np.arange(128)[:, None]
    qi = np.arange(128)[None, :]
    for gc in range(ngc):
        qg = base + 128 * gc
        va = (kr >= qi) & (qg - 256 + kr >= 0)
        vb = np.broadcast_to(qg - 128 + kr >= 0, (128, 128))
        vc = np.broadcast_to(kr <= qi, (128, 128))
        msk[:, gc, 0, :] = np.where(va, 0, NEG).astype(BF)
        msk[:, gc, 1, :] = np.where(vb, 0, NEG).astype(BF)
        msk[:, gc, 2, :] = np.where(vc, 0, NEG).astype(BF)
    return msk


def host_rotary(n, positions):
    half = np.arange(0, 64, 2, dtype=np.float64)
    inv_freq = 1.0 / 10000.0 ** (half / 64.0)
    t = positions.astype(np.float64)
    fr = t[:, None] * inv_freq[None, :]
    fr = np.concatenate([fr, fr], -1)
    cos, sin = np.cos(fr), np.sin(fr)
    bsc = (half + 0.4 * 64) / (1.4 * 64)
    power = (t - n // 2) / (256 // 2)
    sc = bsc[None, :] ** power[:, None]
    sc = np.concatenate([sc, sc], -1)
    return cos, sin, sc


def host_rott(n, positions):
    cos, sin, sc = host_rotary(n, positions)
    inv = 1.0 / sc
    T = len(positions)
    tab = np.zeros((128, 4, T), np.float32)
    for w, (c2, s2) in enumerate([(cos * sc, sin * sc), (cos * inv, sin * inv)]):
        tab[:, 2 * w + 0, :] = np.tile(c2.T.astype(np.float32), (2, 1))
        tab[:, 2 * w + 1, :] = np.tile(s2.T.astype(np.float32), (2, 1))
    return tab


def host_consts():
    cbf = np.zeros((128, 130), BF)
    cbf[:, 0:128] = np.eye(128, dtype=np.float32).astype(BF)
    cbf[0:64, 128] = 1
    cbf[64:128, 129] = 1
    cf32 = np.zeros((128, 192), np.float32)
    cf32[0, 0:64] = 1.0
    cf32[1, 64:128] = 1.0
    cf32[0, 128:192] = 1.0
    cf32[64, 128:192] = 1.0
    pbf = np.zeros((128, 6, 128), np.float32)
    for h in range(2):
        for m in range(128):
            pbf[64 * h + m % 64, h, m] = 1.0
        for m in range(64):
            pbf[64 * h + m, 2 + h, m] = 1.0
        for m in range(64, 128):
            pbf[64 * h + m - 64, 4 + h, m] = 1.0
    pbf = pbf.astype(BF)
    pf32 = np.zeros((128, 4, 128), np.float32)
    for m in range(128):
        base = (m // 64) * 64
        d = m % 64
        src = base + (d + 32 if d < 32 else d - 32)
        pf32[src, 0, m] = -1.0 if d < 32 else 1.0
    for h in range(2):
        for p in range(64):
            pf32[p, 1 + h, 64 * h + p] = 1.0
    return cbf, cf32, pbf, pf32


def host_lvec(NL, b1, b2, lng, lnb, qsl, ksl):
    lv = np.zeros((NL, 128, SL_NV), np.float32)
    for li in range(NL):
        for hc in range(22):
            rows = min(128, 2730 - 128 * hc)
            lv[li, :rows, SL_B1A + hc] = b1[li][128 * hc:128 * hc + rows]
            lv[li, :rows, SL_B1G + hc] = b1[li][2730 + 128 * hc:2730 + 128 * hc + rows]
        for c in range(8):
            lv[li, :, SL_B2 + c] = b2[li][128 * c:128 * (c + 1)]
            lv[li, :, SL_LNG + c] = lng[li][128 * c:128 * (c + 1)]
            lv[li, :, SL_LNB + c] = lnb[li][128 * c:128 * (c + 1)]
        lv[li, :, SL_QS] = np.tile(qsl[li], 2)
        lv[li, :, SL_KS] = np.tile(ksl[li], 2)
        lv[li, :, SL_EPS] = 1e-5
        lv[li, :, SL_TINY] = 1e-30
    return lv


def host_ovec(gam, bet):
    ov = np.zeros((128, 32), np.float32)
    for c in range(8):
        ov[:, OV_G + c] = gam[128 * c:128 * (c + 1)]
        ov[:, OV_B + c] = bet[128 * c:128 * (c + 1)]
    ov[:, OV_EPS] = 1e-5
    return ov


def host_core_inputs(xb, base, g, plan, NL, shared):
    S = plan.S
    W = g.W
    span = xb[base:base + W]
    eb, mz, gt = build_outer_data(plan, base)
    m = dict(shared)
    m.update({
        "xfull": fm(xb),
        "xcore": fm(np.ascontiguousarray(span)),
        "msk_in": host_masks(base, W // 128),
        "rott_in": host_rott(S, np.arange(base, base + W)),
        "ebin": eb, "mzin": mz, "gtin": gt,
    })
    return m


from concourse.bass_utils import run_bass_kernel_spmd

S_FULL = 4096
W_CORE = 3072
N_LAYERS = 6
BASE_L = 0
BASE_R = 1024
SPLIT = 2572

_G = Geo(W=W_CORE, margin=16, tile=512, n_layers=N_LAYERS)
_PLAN = OPlan(_G, S=S_FULL, baseL=BASE_L, baseR=BASE_R)


def kernel(L0_attn_x, L0_attn_ln_gamma, L0_attn_ln_beta, L0_attn_Wq,
           L0_attn_Wk, L0_attn_Wv, L0_attn_Wo, L0_ffn_Wqkv, L0_ffn_q_scale,
           L0_ffn_k_scale, L0_ffn_Wo, L0_ffn_ln_g, L0_ffn_ln_b, L0_ffn_W1,
           L0_ffn_b1, L0_ffn_W2, L0_ffn_b2):
    g, plan = _G, _PLAN
    x = np.ascontiguousarray(np.asarray(L0_attn_x, np.float32))
    cbf, cf32, pbf, pf32 = host_consts()
    shared = {
        "owq": np.ascontiguousarray(L0_attn_Wq, np.float32),
        "owk": np.ascontiguousarray(L0_attn_Wk, np.float32),
        "owv": np.ascontiguousarray(L0_attn_Wv, np.float32),
        "owo": np.ascontiguousarray(L0_attn_Wo, np.float32),
        "fqkv": np.ascontiguousarray(L0_ffn_Wqkv, np.float32),
        "fwo": np.ascontiguousarray(L0_ffn_Wo, np.float32),
        "fw1": np.ascontiguousarray(L0_ffn_W1, np.float32),
        "fw2": np.ascontiguousarray(L0_ffn_W2, np.float32),
        "lvin": host_lvec(N_LAYERS, np.asarray(L0_ffn_b1, np.float32),
                          np.asarray(L0_ffn_b2, np.float32),
                          np.asarray(L0_ffn_ln_g, np.float32),
                          np.asarray(L0_ffn_ln_b, np.float32),
                          np.asarray(L0_ffn_q_scale, np.float32),
                          np.asarray(L0_ffn_k_scale, np.float32)),
        "ovin": host_ovec(np.asarray(L0_attn_ln_gamma, np.float32),
                          np.asarray(L0_attn_ln_beta, np.float32)),
        "cbf_in": cbf, "cf32_in": cf32, "pbf_in": pbf, "pf32_in": pf32,
    }
    role = {}
    for base in (BASE_L, BASE_R):
        eb, mz, gt = build_outer_data(plan, base)
        role[base] = {
            "msk_in": host_masks(base, g.W // 128),
            "rott_in": host_rott(S_FULL, np.arange(base, base + g.W)),
            "ebin": eb, "mzin": mz, "gtin": gt,
        }
    in_maps = []
    for b in range(4):
        xb = x[b]
        xf = fm(xb)
        for base in (BASE_L, BASE_R):
            m = dict(shared)
            m.update(role[base])
            m["xfull"] = xf
            m["xcore"] = np.ascontiguousarray(xf[:, :, base:base + g.W])
            in_maps.append(m)

    nc = build_full(g, plan, N_LAYERS)
    res = run_bass_kernel_spmd(nc, in_maps, core_ids=list(range(8)))

    out = np.empty((4, S_FULL, 1024), np.float32)
    for b in range(4):
        outL = unfm(np.asarray(res.results[2 * b]["outd"]))
        outR = unfm(np.asarray(res.results[2 * b + 1]["outd"]))
        out[b, 0:SPLIT] = outL[0:SPLIT]
        out[b, SPLIT:S_FULL] = outR[SPLIT - BASE_R:S_FULL - BASE_R]
    return out


# version 19
# speedup vs baseline: 1.0482x; 1.0482x over previous
import contextlib
import numpy as np
import ml_dtypes
import concourse.bass as bass
import concourse.mybir as mybir

F32 = mybir.dt.float32
BF16 = mybir.dt.bfloat16
AF = mybir.ActivationFunctionType
OP = mybir.AluOpType
AX = mybir.AxisListType

ENGINES = ("PE", "ACT", "DVE", "SP", "POOL")


def ap_box(ap):
    t = ap.tensor
    fs = 1
    for d in list(t.shape)[1:]:
        fs *= int(d)
    off = int(ap.offset)
    p0, f0 = off // fs, off % fs
    pext = fext = 0
    for step, count in [(int(s), int(c)) for (s, c) in ap.ap]:
        if count <= 1:
            continue
        r = step * (count - 1)
        if step % fs == 0 and step != 0:
            pext += r // fs
        else:
            fext += r
    return (p0, p0 + pext + 1, f0, f0 + fext + 1)


def boxes_overlap(a, b):
    return a[0] < b[1] and b[0] < a[1] and a[2] < b[3] and b[2] < a[3]


class KB:
    def __init__(self, nc, stack: contextlib.ExitStack):
        self.nc = nc
        self.st = stack
        self.eng = {
            "PE": nc.tensor, "ACT": nc.scalar, "DVE": nc.vector,
            "SP": nc.sync, "POOL": nc.gpsimd,
        }
        self.chain = {}
        self.cnt = {}
        for e in ENGINES:
            self.chain[e] = stack.enter_context(nc.semaphore(f"ch_{e}"))
            self.cnt[e] = 0
        self.dsem = {}
        self.dcnt = {}
        for e in ("SP", "ACT", "POOL", "DVE"):
            self.dsem[e] = stack.enter_context(nc.semaphore(f"dma_{e}"))
            self.dcnt[e] = 0
        self.reg = {}
        self.waited = {e: {} for e in ENGINES}
        self.sbuf_bytes = 0
        self._blk = None

    def sb(self, name, shape, dtype):
        h = self.st.enter_context(self.nc.sbuf_tensor(name, list(shape), dtype))
        self.sbuf_bytes += int(np.prod(shape[1:])) * 128 * mybir.dt.size(dtype)
        return h

    def ps(self, name, shape, dtype=F32):
        return self.st.enter_context(self.nc.psum_tensor(name, list(shape), dtype))

    @contextlib.contextmanager
    def block(self, name=None):
        with self.nc.Block(name) as blk:
            self._blk = blk
            self._fns = {}
            yield blk
            self._blk = None
        self.reg = {}
        self.waited = {e: {} for e in ENGINES}

    @contextlib.contextmanager
    def section(self, engine):
        dec = getattr(self._blk, {"PE": "tensor", "ACT": "scalar", "DVE": "vector",
                                  "SP": "sync", "POOL": "gpsimd"}[engine])
        holder = {}

        def body(eng):
            holder["eng"] = eng
            for fn in holder["fns"]:
                fn(eng)

        holder["fns"] = []
        self._cur_engine = engine
        self._cur_fns = holder["fns"]
        yield
        self._cur_engine = None
        self._cur_fns = None
        dec(body)

    def _tokens_for(self, aps, kinds):
        toks = []
        for ap, kind in zip(aps, kinds):
            name = ap.tensor.name
            ent = self.reg.get(name)
            if not ent:
                continue
            box = ap_box(ap)
            for b, tok in ent["w"]:
                if boxes_overlap(box, b):
                    toks.append(tok)
            if kind == "w":
                for b, tok in ent["r"]:
                    if boxes_overlap(box, b):
                        toks.append(tok)
        return toks

    def _record(self, aps, kinds, token):
        for ap, kind in zip(aps, kinds):
            name = ap.tensor.name
            ent = self.reg.setdefault(name, {"w": [], "r": []})
            box = ap_box(ap)
            if kind == "w":
                ent["w"] = [(b, t) for (b, t) in ent["w"]
                            if not (box[0] <= b[0] and b[1] <= box[1]
                                    and box[2] <= b[2] and b[3] <= box[3])]
                ent["w"].append((box, token))
                ent["r"] = [(b, t) for (b, t) in ent["r"]
                            if not (box[0] <= b[0] and b[1] <= box[1]
                                    and box[2] <= b[2] and b[3] <= box[3])]
                if len(ent["w"]) > 48:
                    ent["w"] = self._coarsen(ent["w"])
            else:
                ent["r"].append((box, token))
                if len(ent["r"]) > 48:
                    ent["r"] = self._coarsen(ent["r"])

    @staticmethod
    def _coarsen(lst):
        old, new = lst[:len(lst) // 2], lst[len(lst) // 2:]
        per_sem = {}
        cover = [min(b[0] for b, _ in old), max(b[1] for b, _ in old),
                 min(b[2] for b, _ in old), max(b[3] for b, _ in old)]
        for b, (sem, val) in old:
            key = id(sem)
            if key not in per_sem or per_sem[key][1][1] < val:
                per_sem[key] = (tuple(cover), (sem, val))
        return list(per_sem.values()) + new

    def _emit_waits(self, engine, toks, eng_obj):
        w = self.waited[engine]
        need = {}
        for sem, val in toks:
            key = id(sem)
            if w.get(key, 0) < val and need.get(key, (None, 0))[1] < val:
                need[key] = (sem, val)
        for sem, val in need.values():
            eng_obj.wait_ge(sem, val)
            w[id(sem)] = val

    def _finish(self, engine, ins, aps, kinds, n_inc=1, sem=None):
        if sem is None:
            sem = self.chain[engine]
            self.cnt[engine] += n_inc
            val = self.cnt[engine]
        else:
            raise RuntimeError
        ins.then_inc(sem, n_inc)
        token = (sem, val)
        self._record(aps, kinds, token)
        return token

    def _op(self, engine, fn, aps, kinds):
        toks = self._tokens_for(aps, kinds)

        def thunk(eng):
            self._emit_waits(engine, toks, eng)
            ins = fn(eng)
            sem = self.chain[engine]
            self.cnt[engine] += 1
            ins.then_inc(sem, 1)

        raise RuntimeError("use immediate mode")



class KBI:

    def __init__(self, nc, stack: contextlib.ExitStack):
        self.nc = nc
        self.st = stack
        self.engmap = {"PE": "tensor", "ACT": "scalar", "DVE": "vector",
                       "SP": "sync", "POOL": "gpsimd"}
        self.chain = {}
        self.cnt = {}
        for e in ENGINES:
            self.chain[e] = stack.enter_context(nc.semaphore(f"ch_{e}"))
            self.cnt[e] = 0
        self.DMA_POOL = 8
        self.dsem = {}
        self.dcnt = {}
        self.didx = {}
        for e in ("SP", "ACT", "POOL"):
            self.dsem[e] = [stack.enter_context(nc.semaphore(f"dma_{e}{i}"))
                            for i in range(self.DMA_POOL)]
            self.dcnt[e] = [0] * self.DMA_POOL
            self.didx[e] = 0
        self.reg = {}
        self.waited = {e: {} for e in ENGINES}
        self.sbuf_bytes = 0
        self._sections = None

    def _uname(self, name):
        if not hasattr(self, "_names"):
            self._names = {}
        n = self._names.get(name, 0)
        self._names[name] = n + 1
        return name if n == 0 else f"{name}_{n}"

    def sb(self, name, shape, dtype):
        h = self.st.enter_context(self.nc.sbuf_tensor(self._uname(name), list(shape), dtype))
        self.sbuf_bytes += int(np.prod(shape[1:])) * 128 * mybir.dt.size(dtype)
        return h

    def ps(self, name, shape, dtype=F32):
        return self.st.enter_context(self.nc.psum_tensor(self._uname(name), list(shape), dtype))

    @contextlib.contextmanager
    def block(self, name=None):
        self._ops = {e: [] for e in ENGINES}
        yield
        with self.nc.Block(name) as blk:
            for e in ENGINES:
                ops = self._ops[e]
                if not ops:
                    continue
                dec = getattr(blk, self.engmap[e])

                def body(eng, ops=ops):
                    for f in ops:
                        f(eng)
                dec(body)
        self._ops = None
        dma_ids = {id(s) for pool in self.dsem.values() for s in pool}
        nreg = {}
        for name_, ent in self.reg.items():
            w = [(b, t) for (b, t) in ent["w"] if id(t[0]) in dma_ids]
            r = [(b, t) for (b, t) in ent["r"] if id(t[0]) in dma_ids]
            if w or r:
                nreg[name_] = {"w": w, "r": r}
        self.reg = nreg
        self.waited = {e: {} for e in ENGINES}

    def emit(self, engine, fn, rd=(), wr=(), inc=1):
        rd = list(rd)
        wr = list(wr)
        aps = rd + wr
        kinds = ["r"] * len(rd) + ["w"] * len(wr)
        toks = self._tokens_for(aps, kinds)
        self.cnt[engine] += 1
        val = self.cnt[engine]
        sem = self.chain[engine]
        waits = self._dedup_waits(engine, toks)

        def thunk(eng, fn=fn, waits=waits, sem=sem):
            for s, v in waits:
                eng.wait_ge(s, v)
            ins = fn(eng)
            ins.then_inc(sem, 1)
        self._ops[engine].append(thunk)
        token = (sem, val)
        self._record(aps, kinds, token)
        return token

    def raw(self, engine, fn):
        self._ops[engine].append(fn)

    def drain_dmas(self):
        for qe in ("SP", "ACT", "POOL"):
            for i in range(self.DMA_POOL):
                c = self.dcnt[qe][i]
                if c:
                    self.wait_token(qe, (self.dsem[qe][i], c))

    def wait_token(self, engine, token):
        waits = self._dedup_waits(engine, [token])

        def thunk(eng, waits=waits):
            for s, v in waits:
                eng.wait_ge(s, v)
        self._ops[engine].append(thunk)

    def _dedup_waits(self, engine, toks):
        w = self.waited[engine]
        need = {}
        for sem, val in toks:
            key = id(sem)
            if w.get(key, 0) < val and need.get(key, (None, 0))[1] < val:
                need[key] = (sem, val)
        for key, (sem, val) in need.items():
            w[key] = val
        return list(need.values())

    _tokens_for = KB._tokens_for
    _record = KB._record
    _coarsen = staticmethod(KB._coarsen)

    def dma(self, qeng, out_ap, in_ap):
        rd, wr = [in_ap], [out_ap]
        aps = rd + wr
        kinds = ["r", "w"]
        toks = self._tokens_for(aps, kinds)
        i = self.didx[qeng]
        self.didx[qeng] = (i + 1) % self.DMA_POOL
        sem = self.dsem[qeng][i]
        prev = self.dcnt[qeng][i]
        if prev:
            toks = toks + [(sem, prev)]
        self.dcnt[qeng][i] = prev + 16
        val = prev + 16
        waits = self._dedup_waits(qeng, toks)

        def thunk(eng, waits=waits, sem=sem, out_ap=out_ap, in_ap=in_ap):
            for s, v in waits:
                eng.wait_ge(s, v)
            eng.dma_start(out_ap, in_ap).then_inc(sem, 16)
        self._ops[qeng].append(thunk)
        token = (sem, val)
        self._record(aps, kinds, token)
        return token

    def tt(self, engine, out, a, b, op):
        return self.emit(engine, lambda e: e.tensor_tensor(out, a, b, op),
                         rd=[a, b], wr=[out])

    def ts(self, engine, out, a, s1, s2, op0, op1=None, accum_out=None):
        rd = [a] + [x for x in (s1, s2) if isinstance(x, bass.AP)]
        wr = [out] + ([accum_out] if accum_out is not None else [])
        if op1 is None:
            return self.emit(engine, lambda e: e.tensor_scalar(out, a, s1, s2, op0, accum_out=accum_out),
                             rd=rd, wr=wr)
        return self.emit(engine, lambda e: e.tensor_scalar(out, a, s1, s2, op0, op1, accum_out=accum_out),
                         rd=rd, wr=wr)

    def stt(self, out, a, s, b, op0, op1):
        rd = [a, b] + ([s] if isinstance(s, bass.AP) else [])
        return self.emit("DVE", lambda e: e.scalar_tensor_tensor(out, a, s, b, op0, op1),
                         rd=rd, wr=[out])

    def cp(self, engine, out, a):
        if engine == "ACT":
            return self.emit(engine, lambda e: e.copy(out, a), rd=[a], wr=[out])
        return self.emit(engine, lambda e: e.tensor_copy(out, a), rd=[a], wr=[out])

    def act(self, out, a, func, bias=None, scale=None, accum_out=None):
        kw = {}
        if bias is not None:
            kw["bias"] = bias
        if scale is not None:
            kw["scale"] = scale
        if accum_out is not None:
            kw["accum_out"] = accum_out
        rd = [a] + [x for x in (bias, scale) if isinstance(x, bass.AP)]
        wr = [out] + ([accum_out] if accum_out is not None else [])
        return self.emit("ACT", lambda e: e.activation(out, a, func, **kw), rd=rd, wr=wr)

    def memset(self, engine, ap, value):
        return self.emit(engine, lambda e: e.memset(ap, value), rd=[], wr=[ap])

    def reduce(self, out, a, op, negate=False):
        return self.emit("DVE", lambda e: e.tensor_reduce(out, a, AX.X, op, negate=negate),
                         rd=[a], wr=[out])

    def recip(self, out, a):
        return self.emit("DVE", lambda e: e.reciprocal(out, a), rd=[a], wr=[out])

    def mm(self, passes, out):
        rd = []
        for l, r in passes:
            rd += [l, r]
        aps = rd + [out]
        kinds = ["r"] * len(rd) + ["w"]
        out_toks = self._tokens_for([out], ["w"])
        self.cnt["PE"] += 1
        val = self.cnt["PE"]
        sem = self.chain["PE"]
        pw = []
        for i, (l, r) in enumerate(passes):
            t = self._tokens_for([l, r], ["r", "r"])
            if i == 0:
                t = out_toks + t
            pw.append(self._dedup_waits("PE", t))
        n = len(passes)

        def thunk(eng, passes=passes, out=out, pw=pw, sem=sem, n=n):
            ins = None
            for i, (l, r) in enumerate(passes):
                for sv, v in pw[i]:
                    eng.wait_ge(sv, v)
                ins = eng.matmul(out, l, r, start=(i == 0), stop=(i == n - 1))
            ins.then_inc(sem, 1)
        self._ops["PE"].append(thunk)
        token = (sem, val)
        self._record(aps, kinds, token)
        return token

    def mm_acc(self, passes, out, start, stop):
        rd = []
        for l, r in passes:
            rd += [l, r]
        aps = rd + [out]
        kinds = ["r"] * len(rd) + ["w"]
        out_toks = self._tokens_for([out], ["w"])
        self.cnt["PE"] += 1
        val = self.cnt["PE"]
        sem = self.chain["PE"]
        pw = []
        for i, (l, r) in enumerate(passes):
            t = self._tokens_for([l, r], ["r", "r"])
            if i == 0:
                t = out_toks + t
            pw.append(self._dedup_waits("PE", t))
        n = len(passes)

        def thunk(eng, passes=passes, out=out, pw=pw, sem=sem, n=n):
            ins = None
            for i, (l, r) in enumerate(passes):
                for sv, v in pw[i]:
                    eng.wait_ge(sv, v)
                ins = eng.matmul(out, l, r, start=(start and i == 0),
                                 stop=(stop and i == n - 1))
            ins.then_inc(sem, 1)
        self._ops["PE"].append(thunk)
        token = (sem, val)
        self._record(aps, kinds, token)
        return token


def hi_lo_split(kb, h_ap, l_ap, f32_ap, eng="DVE"):
    kb.cp(eng, h_ap, f32_ap)
    kb.tt(eng, l_ap, f32_ap, h_ap, OP.subtract)


import contextlib
import numpy as np
import ml_dtypes
import concourse.bass as bass
import concourse.mybir as mybir

BF = ml_dtypes.bfloat16
STATS_FOLD = False
INV_D = 1.0 / 1024.0

SL_B1A = 0
SL_B1G = 22
SL_B2 = 44
SL_LNG = 52
SL_LNB = 60
SL_QS = 68
SL_KS = 69
SL_EPS = 70
SL_NV = 80


class Geo:
    D = 1024
    FC = 8
    H = 16
    DH = 64
    WIN = 256
    DI = 2730
    NHC = 22
    LN_EPS = 1e-5

    def __init__(self, W=3072, margin=272, tile=512, n_layers=6, st=None):
        self.W = W
        self.M = margin
        self.TT = tile
        self.NL = n_layers
        self.st = st if st is not None else [0] * n_layers

    def tiles(self, li):
        out = []
        c = self.st[li]
        while c < self.W:
            e = min(c + self.TT, self.W)
            out.append((c, e))
            c = e
        return out


def hl(x):
    h = np.asarray(x, np.float32).astype(BF)
    l = (np.asarray(x, np.float32) - h.astype(np.float32)).astype(BF)
    return h, l


def fm(x):
    T, D = x.shape
    return np.ascontiguousarray(x.T.reshape(D // 128, 128, T).transpose(1, 0, 2))


def unfm(x):
    p, fc, T = x.shape
    return np.ascontiguousarray(x.transpose(1, 0, 2).reshape(fc * 128, T).T)


class Bufs:
    pass


def alloc_wbufs(kb):
    bw = Bufs()
    bw.wh = [kb.sb(f"wh{i}", [128, 24, 128], BF16) for i in range(2)]
    bw.wl = [kb.sb(f"wl{i}", [128, 24, 128], BF16) for i in range(2)]
    bw.w32 = [kb.sb(f"w32_{i}", [128, 4, 128], F32) for i in range(2)]
    bw.w32i = [0]
    return bw


def alloc_inner_bufs(kb, g: Geo, bw=None):
    b = Bufs()
    TT = g.TT
    if bw is not None:
        b.wh, b.wl, b.w32 = bw.wh, bw.wl, bw.w32
        b.w32i = bw.w32i
    b.yh = kb.sb("yh", [128, g.FC, TT], BF16)
    b.yl = kb.sb("yl", [128, g.FC, TT], BF16)
    b.th = kb.sb("th", [128, g.FC, TT], BF16)
    b.tl = kb.sb("tl", [128, g.FC, TT], BF16)
    b.hsv = kb.sb("hsv", [128, 2, g.FC], BF16)
    b.hsv2 = kb.sb("hsv2", [128, 2, g.FC], BF16)
    b.hsnap = kb.sb("hsnap", [128, 2, g.FC], BF16)
    if bw is None:
        b.wh = [kb.sb(f"wh{i}", [128, 24, 128], BF16) for i in range(2)]
        b.wl = [kb.sb(f"wl{i}", [128, 24, 128], BF16) for i in range(2)]
        b.w32 = [kb.sb(f"w32_{i}", [128, 4, 128], F32) for i in range(2)]
        b.w32i = [0]
    b.t32 = kb.sb("t32", [128, TT], F32)
    b.sq32 = kb.sb("sq32", [128, TT], F32)
    kb._shared_t32 = b.t32
    kb._shared_sq32 = b.sq32
    b.sqh = kb.sb("sqh", [128, TT], BF16)
    b.sql = kb.sb("sql", [128, TT], BF16)
    b.muR = kb.sb("muR", [128, TT], F32)
    kb._shared_muR = b.muR
    b.m2 = kb.sb("m2", [128, TT], F32)
    b.varf = b.m2
    b.sdf = b.m2
    b.rR = kb.sb("rR", [128, TT], F32)
    b.u32 = kb.sb("u32", [128, TT], F32)
    b.x32 = b.u32
    kb._shared_u32 = b.u32
    b.gth = kb.sb("gth", [128, 11, TT], BF16)
    b.gtl = kb.sb("gtl", [128, 11, TT], BF16)
    b.w2acc = kb.sb("w2acc", [128, 4, TT], F32)
    b.ge32 = kb.sb("ge32", [128, TT], F32)
    b.gg32 = b.ge32
    b.y32 = kb.sb("y32", [128, TT], F32)
    b.r132 = b.y32
    kb._shared_y32 = b.y32
    b.ones128 = kb.sb("ones128", [128, 128], BF16)
    b.lvec = kb.sb("lvec", [128, SL_NV], F32)
    return b


def t_view(b, g, c, col, T, which):
    y = b.yh if which == "h" else b.yl
    off = col - 1 if c >= 4 else col
    return y[:, c, off:off + T]


def emit_build_t(kb, b, g, T, hsv, c_from=0):
    for c in range(c_from, g.FC):
        if c < 4:
            kb.cp("DVE", b.th[:, c, 0:T], b.yh[:, c, 0:T])
            kb.cp("DVE", b.tl[:, c, 0:T], b.yl[:, c, 0:T])
        else:
            kb.cp("DVE", b.th[:, c, 0:1], hsv[:, 0, c:c + 1])
            kb.cp("DVE", b.tl[:, c, 0:1], hsv[:, 1, c:c + 1])
            kb.cp("DVE", b.th[:, c, 1:T], b.yh[:, c, 0:T - 1])
            kb.cp("DVE", b.tl[:, c, 1:T], b.yl[:, c, 0:T - 1])
    for c in range(c_from, g.FC):
        kb.cp("DVE", hsv[:, 0, c:c + 1], b.yh[:, c, T - 1:T])
        kb.cp("DVE", hsv[:, 1, c:c + 1], b.yl[:, c, T - 1:T])
    if getattr(b, "cm_col", None) is not None:
        kb.ts("DVE", hsv[:, :, :], hsv[:, :, :], b.cm_col, None, OP.mult)


def emit_hsv_init(kb, b, g, hsv):
    kb.cp("DVE", hsv[:, :, :], b.hsnap[:, :, :])


def emit_ln_stats(kb, b, g, ps, T, src_h, src_l, eps=1e-5):
    s1 = ps["E"]
    s2 = ps["F"]
    passes = []
    for c in range(g.FC):
        passes += [(b.ones128[:, :], src_h(c)), (b.ones128[:, :], src_l(c))]
    kb.mm(passes, s1[:, 0:T])
    for c in range(g.FC):
        kb.tt("DVE", b.t32[:, 0:T], src_h(c), src_l(c), OP.add)
        kb.act(b.sq32[:, 0:T], b.t32[:, 0:T], AF.Square)
        kb.cp("DVE", b.sqh[:, 0:T], b.sq32[:, 0:T])
        kb.tt("DVE", b.sql[:, 0:T], b.sq32[:, 0:T], b.sqh[:, 0:T], OP.subtract)
        kb.mm_acc([(b.ones128[:, :], b.sqh[:, 0:T]),
                   (b.ones128[:, :], b.sql[:, 0:T])],
                  s2[:, 0:T], start=(c == 0), stop=(c == g.FC - 1))
    emit_ln_finalize(kb, b, g, ps, T)


def emit_ln_finalize(kb, b, g, ps, T):
    kb.ts("DVE", b.muR[:, 0:T], ps["E"][:, 0:T], INV_D, None, OP.mult)
    kb.tt("DVE", b.m2[:, 0:T], b.muR[:, 0:T], b.muR[:, 0:T], OP.mult)
    kb.stt(b.varf[:, 0:T], ps["F"][:, 0:T], INV_D, b.m2[:, 0:T],
           OP.mult, OP.subtract)
    kb.act(b.sdf[:, 0:T], b.varf[:, 0:T], AF.Sqrt,
           bias=b.lvec[:, SL_EPS:SL_EPS + 1])
    kb.recip(b.rR[:, 0:T], b.sdf[:, 0:T])


def emit_ffn_sublayer(kb, b, g, ps, li, wd, c0, c1, pre_done=False):
    T = c1 - c0
    lv = b.lvec

    if pre_done:
        if getattr(b, "cm_col", None) is not None:
            kb.ts("DVE", b.hsv2[:, :, :], b.hsv2[:, :, :], b.cm_col, None,
                  OP.mult)
        emit_ln_finalize(kb, b, g, ps, T)
    else:
        emit_build_t(kb, b, g, T, b.hsv2)
        emit_ln_stats(kb, b, g, ps, T,
                      lambda c: b.th[:, c, 0:T], lambda c: b.tl[:, c, 0:T],
                      eps=g.LN_EPS)
    for c in range(g.FC):
        kb.tt("DVE", b.t32[:, 0:T], b.th[:, c, 0:T], b.tl[:, c, 0:T], OP.add)
        kb.tt("DVE", b.u32[:, 0:T], b.t32[:, 0:T], b.muR[:, 0:T], OP.subtract)
        kb.tt("DVE", b.x32[:, 0:T], b.u32[:, 0:T], b.rR[:, 0:T], OP.mult)
        kb.ts("DVE", b.x32[:, 0:T], b.x32[:, 0:T],
              lv[:, SL_LNG + c:SL_LNG + c + 1], lv[:, SL_LNB + c:SL_LNB + c + 1],
              OP.mult, OP.add)
        kb.cp("ACT", b.th[:, c, 0:T], b.x32[:, 0:T])
        kb.tt("DVE", b.tl[:, c, 0:T], b.x32[:, 0:T], b.th[:, c, 0:T], OP.subtract)

    for grp in range(2):
        for j in range(11):
            hc = 11 * grp + j
            rows = min(128, g.DI - 128 * hc)
            bi = j % 2
            pA = ps["A"] if j % 2 == 0 else ps["G"]
            pB = ps["B"] if j % 2 == 0 else ps["H"]
            emit_w_load_pre(kb, b, wd, bi, 0, WS_W1 + 16 * hc, 16)
            pa = [(b.wh[bi][:, kc, 0:rows], b.th[:, kc, 0:T]) for kc in range(8)]
            pa += [(b.wh[bi][:, kc, 0:rows], b.tl[:, kc, 0:T]) for kc in range(8)]
            pa += [(b.wl[bi][:, kc, 0:rows], b.th[:, kc, 0:T]) for kc in range(8)]
            kb.mm(pa, pA[0:rows, 0:T])
            pg = [(b.wh[bi][:, 8 + kc, 0:rows], b.th[:, kc, 0:T]) for kc in range(8)]
            pg += [(b.wh[bi][:, 8 + kc, 0:rows], b.tl[:, kc, 0:T]) for kc in range(8)]
            pg += [(b.wl[bi][:, 8 + kc, 0:rows], b.th[:, kc, 0:T]) for kc in range(8)]
            kb.mm(pg, pB[0:rows, 0:T])
            kb.act(b.ge32[0:rows, 0:T], pB[0:rows, 0:T], AF.Gelu,
                   bias=lv[0:rows, SL_B1G + hc:SL_B1G + hc + 1])
            kb.stt(b.gg32[0:rows, 0:T], pA[0:rows, 0:T],
                   lv[0:rows, SL_B1A + hc:SL_B1A + hc + 1], b.ge32[0:rows, 0:T],
                   OP.add, OP.mult)
            kb.cp("ACT", b.gth[0:rows, j, 0:T], b.gg32[0:rows, 0:T])
            kb.tt("DVE", b.gtl[0:rows, j, 0:T], b.gg32[0:rows, 0:T],
                  b.gth[0:rows, j, 0:T], OP.subtract)
        obank = [ps["C"], ps["D"], ps["E"], ps["F"]]
        for oc in range(g.FC):
            bi = oc % 2
            emit_w_load_pre(kb, b, wd, bi, 0, WS_W2 + 88 * grp + 11 * oc, 11)
            pw = []
            for j in range(11):
                hc = 11 * grp + j
                rows = min(128, g.DI - 128 * hc)
                pw += [(b.wh[bi][0:rows, j, :], b.gth[0:rows, j, 0:T]),
                       (b.wh[bi][0:rows, j, :], b.gtl[0:rows, j, 0:T]),
                       (b.wl[bi][0:rows, j, :], b.gth[0:rows, j, 0:T])]
            if oc < 4:
                kb.mm_acc(pw, obank[oc][:, 0:T], start=(grp == 0), stop=(grp == 1))
                if grp == 0:
                    continue
                src_ps = obank[oc]
                kb.stt(b.r132[:, 0:T], src_ps[:, 0:T],
                       lv[:, SL_B2 + oc:SL_B2 + oc + 1], b.yh[:, oc, 0:T],
                       OP.add, OP.add)
            else:
                pX = [ps["A"], ps["B"], ps["G"], ps["H"]][oc - 4]
                kb.mm(pw, pX[:, 0:T])
                if grp == 0:
                    kb.cp("ACT", b.w2acc[:, oc - 4, 0:T], pX[:, 0:T])
                    continue
                kb.tt("DVE", b.r132[:, 0:T], pX[:, 0:T],
                      b.w2acc[:, oc - 4, 0:T], OP.add)
                kb.stt(b.r132[:, 0:T], b.r132[:, 0:T],
                       lv[:, SL_B2 + oc:SL_B2 + oc + 1], b.yh[:, oc, 0:T],
                       OP.add, OP.add)
            kb.tt("DVE", b.y32[:, 0:T], b.r132[:, 0:T],
                  b.yl[:, oc, 0:T], OP.add)
            kb.cp("ACT", b.yh[:, oc, 0:T], b.y32[:, 0:T])
            kb.tt("DVE", b.yl[:, oc, 0:T], b.y32[:, 0:T],
                  b.yh[:, oc, 0:T], OP.subtract)


def emit_w_load(kb, b, g, wdram, o0, ow, bi, slot0=0, kcs=8):
    src = wdram[:, o0:o0 + ow].rearrange("(c p) o -> p c o", p=128)
    kb.dma("POOL", b.wh[bi][:, slot0:slot0 + kcs, 0:ow], src)
    for s0 in range(0, kcs, 4):
        n = min(4, kcs - s0)
        wi = b.w32i[0] % 2
        b.w32i[0] += 1
        kb.dma("SP", b.w32[wi][:, 0:n, 0:ow], src[:, s0:s0 + n, :])
        kb.tt("DVE", b.wl[bi][:, slot0 + s0:slot0 + s0 + n, 0:ow],
              b.w32[wi][:, 0:n, 0:ow],
              b.wh[bi][:, slot0 + s0:slot0 + s0 + n, 0:ow], OP.subtract)


def emit_w2_load(kb, b, g, w2dram, oc, bi, grp):
    r0 = 11 * 128 * grp
    nfull = 11 if grp == 0 else 10
    src = w2dram[r0:r0 + nfull * 128, 128 * oc:128 * (oc + 1)].rearrange(
        "(c p) o -> p c o", p=128)
    kb.dma("POOL", b.wh[bi][:, 0:nfull, :], src)
    for s0 in range(0, nfull, 4):
        n = min(4, nfull - s0)
        wi = b.w32i[0] % 2
        b.w32i[0] += 1
        kb.dma("SP", b.w32[wi][:, 0:n, :], src[:, s0:s0 + n, :])
        kb.tt("DVE", b.wl[bi][:, s0:s0 + n, :], b.w32[wi][:, 0:n, :],
              b.wh[bi][:, s0:s0 + n, :], OP.subtract)
    if grp == 1:
        src2 = w2dram[r0 + 10 * 128:g.DI, 128 * oc:128 * (oc + 1)]
        wi = b.w32i[0] % 2
        b.w32i[0] += 1
        kb.dma("SP", b.w32[wi][0:42, 0, :], src2)
        kb.dma("POOL", b.wh[bi][0:42, 10, :], src2)
        kb.tt("DVE", b.wl[bi][0:42, 10, :], b.w32[wi][0:42, 0, :],
              b.wh[bi][0:42, 10, :], OP.subtract)


SL_TINY = 71

WS_QKV = 0
WS_WO = 192
WS_W1 = 256
WS_W2 = 608
WS_N = 784


def alloc_wsplit_bufs(kb):
    sp = Bufs()
    sp.s32 = [kb.sb(f"wsp32_{i}", [128, 4, 128], F32) for i in range(2)]
    sp.sh = kb.sb("wsph", [128, 4, 4, 128], BF16)
    sp.sl = kb.sb("wspl", [128, 4, 4, 128], BF16)
    sp.i32 = [0]
    return sp


def _ws_chunk(kb, sp, src, n, ow, dsth, dstl):
    i = sp.i32[0] % 2
    j = sp.i32[0] % 4
    sp.i32[0] += 1
    kb.dma("SP", sp.s32[i][:, 0:n, 0:ow], src)
    kb.cp("ACT", sp.sh[:, j, 0:n, 0:ow], sp.s32[i][:, 0:n, 0:ow])
    kb.tt("DVE", sp.sl[:, j, 0:n, 0:ow], sp.s32[i][:, 0:n, 0:ow],
          sp.sh[:, j, 0:n, 0:ow], OP.subtract)
    kb.dma("ACT", dsth, sp.sh[:, j, 0:n, :])
    kb.dma("POOL", dstl, sp.sl[:, j, 0:n, :])


def _ws_chunk1(kb, sp, src2d, np_, ow, dsth, dstl):
    i = sp.i32[0] % 2
    j = sp.i32[0] % 4
    sp.i32[0] += 1
    kb.dma("SP", sp.s32[i][0:np_, 0, 0:ow], src2d)
    kb.cp("ACT", sp.sh[0:np_, j, 0, 0:ow], sp.s32[i][0:np_, 0, 0:ow])
    kb.tt("DVE", sp.sl[0:np_, j, 0, 0:ow], sp.s32[i][0:np_, 0, 0:ow],
          sp.sh[0:np_, j, 0, 0:ow], OP.subtract)
    kb.dma("ACT", dsth, sp.sh[:, j, 0, :])
    kb.dma("POOL", dstl, sp.sl[:, j, 0, :])


def _ws_run(kb, sp, whd_l, wld_l, wsrc, o0, ow, d0):
    nsl = wsrc.shape[0] // 128
    src = wsrc[:, o0:o0 + ow].rearrange("(c p) o -> p c o", p=128)
    for s0 in range(0, nsl, 4):
        n = min(4, nsl - s0)
        d = d0 + s0
        _ws_chunk(kb, sp, src[:, s0:s0 + n, 0:ow], n, ow,
                  whd_l[:, d:d + n, :], wld_l[:, d:d + n, :])


def ws_layer_thunks(kb, sp, g, fqkv_l, fwo_l, fw1_l, fw2_l, whd_l, wld_l):
    th = []

    def run(wsrc, o0, ow, d0):
        nsl = wsrc.shape[0] // 128
        src = wsrc[:, o0:o0 + ow].rearrange("(c p) o -> p c o", p=128)
        for s0 in range(0, nsl, 4):
            n = min(4, nsl - s0)
            d = d0 + s0
            th.append(lambda sv=src[:, s0:s0 + n, 0:ow], n=n, ow=ow,
                      dh=whd_l[:, d:d + n, :], dl=wld_l[:, d:d + n, :]:
                      _ws_chunk(kb, sp, sv, n, ow, dh, dl))

    for hp in range(8):
        for p3 in range(3):
            run(fqkv_l, 1024 * p3 + 128 * hp, 128, WS_QKV + 24 * hp + 8 * p3)
    for oc in range(8):
        run(fwo_l, 128 * oc, 128, WS_WO + 8 * oc)
    for hc in range(22):
        rows = min(128, g.DI - 128 * hc)
        run(fw1_l, 128 * hc, rows, WS_W1 + 16 * hc)
        run(fw1_l, g.DI + 128 * hc, rows, WS_W1 + 16 * hc + 8)
    for grp in range(2):
        for oc in range(8):
            r0 = 11 * 128 * grp
            nf = 11 if grp == 0 else 10
            src = fw2_l[r0:r0 + nf * 128,
                        128 * oc:128 * (oc + 1)].rearrange(
                "(c p) o -> p c o", p=128)
            d0 = WS_W2 + 88 * grp + 11 * oc
            for s0 in range(0, nf, 4):
                n = min(4, nf - s0)
                th.append(lambda sv=src[:, s0:s0 + n, :], n=n,
                          dh=whd_l[:, d0 + s0:d0 + s0 + n, :],
                          dl=wld_l[:, d0 + s0:d0 + s0 + n, :]:
                          _ws_chunk(kb, sp, sv, n, 128, dh, dl))
            if grp == 1:
                src2 = fw2_l[r0 + 10 * 128:g.DI, 128 * oc:128 * (oc + 1)]
                th.append(lambda sv=src2, dh=whd_l[:, d0 + 10, :],
                          dl=wld_l[:, d0 + 10, :]:
                          _ws_chunk1(kb, sp, sv, 42, 128, dh, dl))
    return th


def emit_weight_split(kb, sp, g, fqkv_l, fwo_l, fw1_l, fw2_l, whd_l, wld_l):
    for t in ws_layer_thunks(kb, sp, g, fqkv_l, fwo_l, fw1_l, fw2_l,
                             whd_l, wld_l):
        t()


def emit_w_load_pre(kb, b, wd, bi, slot0, s0, nsl):
    kb.dma("SP", b.wh[bi][:, slot0:slot0 + nsl, :],
           wd["whd"][:, s0:s0 + nsl, :])
    kb.dma("POOL", b.wl[bi][:, slot0:slot0 + nsl, :],
           wd["wld"][:, s0:s0 + nsl, :])


def alloc_attn_bufs(kb, g: Geo):
    a = Bufs()
    TT = g.TT
    NTC = (256 + TT) // 128
    a.khl = kb.sb("khl", [128, g.H, 256 + TT], BF16)
    a.vbh = kb.sb("vbh", [128, NTC, 1040], BF16)
    a.vbl = kb.sb("vbl", [128, NTC, 1040], BF16)
    a.qhh = kb.sb("qhh", [128, 2, TT], BF16)
    a.qll = kb.sb("qll", [128, 2, TT], BF16)
    a.q32 = kb._shared_sq32
    a.t1 = kb._shared_t32
    a.wv = kb._shared_u32
    a.qp32 = kb._shared_y32
    a.qf32 = a.qp32
    a.rqk = kb.sb("arqk", [4, TT], F32)
    a.sdq = a.rqk
    a.p32 = kb.sb("ap32", [128, 384], F32)
    a.ph = kb.sb("aph", [128, 2, 384], BF16)
    a.pl = kb.sb("apl", [128, 2, 384], BF16)
    a.aoh = kb.sb("aaoh", [128, 8, TT], BF16)
    a.aol = kb.sb("aaol", [128, 8, TT], BF16)
    a.rott = kb.sb("arott", [128, 4, TT], F32)
    a.msk = kb.sb("amsk", [128, g.TT // 128, 3, 128], BF16)
    a.colsel = kb.sb("acolsel", [128, 2], BF16)
    a.selpair = kb.sb("aselpair", [2, 128], F32)
    a.ones65 = kb.sb("aones65", [65, 64], F32)
    a.ident = kb.sb("aident", [128, 128], BF16)
    a.pbf = kb.sb("apbf", [128, 6, 128], BF16)
    a.pf32 = kb.sb("apf32", [128, 4, 128], F32)
    a.qhb = kb.sb("aqhb", [128, TT], BF16)
    a.qlb = kb.sb("aqlb", [128, TT], BF16)
    a.invd = kb.sb("ainvd", [65, 2, TT], F32)
    a.an32 = kb._shared_muR[0:64, :]
    return a


def emit_attn_consts(kb, b, a, g, cbf, cf32, pbf, pf32):
    kb.dma("SP", a.ident[:, :], cbf[:, 0:128])
    kb.dma("SP", a.colsel[:, :], cbf[:, 128:130])
    kb.dma("SP", a.selpair[:, :], cf32[0:2, 0:128])
    kb.dma("SP", a.ones65[64:65, :], cf32[64:65, 128:192])
    kb.dma("SP", a.pbf[:, :, :], pbf)
    kb.dma("SP", a.pf32[:, :, :], pf32)
    kb.memset("DVE", a.vbh[:, :, :], 0.0)
    kb.memset("DVE", a.vbl[:, :, :], 0.0)
    for h in range(g.H):
        kb.memset("DVE", a.vbh[:, :, 65 * h + 64:65 * h + 65], 1.0)


def emit_qk_pipe_s(kb, b, a, g, ps_in, ps_rq, ps_rot, T, sc_slot, q32):
    lv = b.lvec
    kb.act(b.sq32[:, 0:T], ps_in[:, 0:T], AF.Square)
    kb.cp("DVE", b.sqh[:, 0:T], b.sq32[:, 0:T])
    kb.tt("DVE", b.sql[:, 0:T], b.sq32[:, 0:T], b.sqh[:, 0:T], OP.subtract)
    kb.mm([(a.colsel[:, :], b.sqh[:, 0:T]),
           (a.colsel[:, :], b.sql[:, 0:T])], ps_rq[0:2, 0:T])
    kb.act(q32[:, 0:T], ps_in[:, 0:T], AF.Copy, scale=lv[:, sc_slot:sc_slot + 1])
    kb.mm([(a.pf32[:, 0, :], q32[:, 0:T])], ps_rot[:, 0:T])


def emit_qk_pipe_r(kb, b, a, g, ps_rq, ps_rot, T, is_k, q32, qp32, sdq):
    lv = b.lvec
    ct = a.rott[:, (2 if is_k else 0), 0:T]
    st = a.rott[:, (3 if is_k else 1), 0:T]
    kb.act(sdq[0:2, 0:T], ps_rq[0:2, 0:T], AF.Sqrt,
           bias=lv[0:2, SL_TINY:SL_TINY + 1])
    if is_k:
        kb.ts("DVE", sdq[0:2, 0:T], sdq[0:2, 0:T], 0.125,
              1.25e-13, OP.mult, OP.max)
    else:
        kb.ts("DVE", sdq[0:2, 0:T], sdq[0:2, 0:T], 1e-12, None,
              OP.max)
    kb.recip(sdq[0:2, 0:T], sdq[0:2, 0:T])
    kb.tt("DVE", a.t1[:, 0:T], q32[:, 0:T], ct, OP.mult)
    kb.tt("DVE", a.wv[:, 0:T], ps_rot[:, 0:T], st, OP.mult)
    kb.tt("DVE", qp32[:, 0:T], a.t1[:, 0:T], a.wv[:, 0:T], OP.add)


def emit_qk_pipe_b(kb, b, a, g, ps_bc, T, qp32, rqk):
    kb.mm([(a.selpair[:, :], rqk[0:2, 0:T])], ps_bc[:, 0:T])
    kb.tt("DVE", qp32[:, 0:T], qp32[:, 0:T], ps_bc[:, 0:T], OP.mult)
    kb.cp("ACT", a.qhb[:, 0:T], qp32[:, 0:T])
    kb.tt("DVE", a.qlb[:, 0:T], qp32[:, 0:T], a.qhb[:, 0:T], OP.subtract)


def emit_dup_forms(kb, a, ps_hi, ps_lo, T, hh, dst_hh, dst_ll):
    kb.mm([(a.pbf[:, hh, :], a.qhb[:, 0:T])], ps_hi[:, 0:T])
    kb.mm([(a.pbf[:, hh, :], a.qlb[:, 0:T])], ps_lo[:, 0:T])
    kb.cp("DVE", dst_hh, ps_hi[:, 0:T])
    kb.cp("DVE", dst_ll, ps_lo[:, 0:T])


def emit_stack_form(kb, a, ps_pk, T, hh, dst):
    kb.mm([(a.pbf[:, 2 + hh, :], a.qhb[:, 0:T]),
           (a.pbf[:, 4 + hh, :], a.qlb[:, 0:T])], ps_pk[:, 0:T])
    kb.cp("DVE", dst, ps_pk[:, 0:T])


def emit_attn_sublayer(kb, b, a, g, ps, li, wd, c0, c1, post_oc=None):
    T = c1 - c0
    NQ = T // 128
    gc0 = c0 // 128
    kb.dma("SP", a.rott[:, :, 0:T], wd["rott"][:, :, c0:c1])
    kb.dma("SP", a.msk[:, 0:T // 128, :, :], wd["mskin"][:, gc0:gc0 + T // 128, :, :])
    emit_build_t(kb, b, g, T, b.hsv, c_from=4)

    def thv(kc):
        return b.yh[:, kc, 0:T] if kc < 4 else b.th[:, kc, 0:T]

    def tlv(kc):
        return b.yl[:, kc, 0:T] if kc < 4 else b.tl[:, kc, 0:T]

    def qsel(hp):
        if hp % 2 == 0:
            return a.qhh, a.qll
        return b.gth, b.gtl

    if c0 == 0:
        kb.memset("DVE", a.khl[:, :, 0:256], 0.0)
        for tc in range(2):
            kb.memset("DVE", a.vbh[:, tc, :], 0.0)
            kb.memset("DVE", a.vbl[:, tc, :], 0.0)
            for h in range(g.H):
                kb.memset("DVE", a.vbh[:, tc, 65 * h + 64:65 * h + 65], 1.0)
    else:
        kb.cp("DVE", a.khl[:, :, 0:256], a.khl[:, :, T:T + 256])
        kb.cp("DVE", a.vbh[:, 0:2, :], a.vbh[:, T // 128:T // 128 + 2, :])
        kb.cp("DVE", a.vbl[:, 0:2, :], a.vbl[:, T // 128:T // 128 + 2, :])

    def attn_pair(hp):
        qh_, ql_ = qsel(hp)
        avb = {0: ps["H"], 1: ps["G"]}
        pend_av = None
        for hh in range(2):
            h = 2 * hp + hh
            for qs in range(NQ):
                it = hh * NQ + qs
                sc = [ps["D"], ps["E"]][it % 2]
                pb = it % 2
                for ci in range(3):
                    kcol = 128 * qs + 128 * ci
                    kb.mm([(a.khl[:, h, kcol:kcol + 128], qh_[:, hh, 128 * qs:128 * (qs + 1)]),
                           (a.khl[:, h, kcol:kcol + 128], ql_[:, hh, 128 * qs:128 * (qs + 1)]),
                           (a.ident[:, :], a.msk[:, qs, ci, :])],
                          sc[:, 128 * ci:128 * ci + 128])
                kb.act(a.p32[:, :], sc[:, 0:384], AF.Exp)
                kb.cp("ACT", a.ph[:, pb, :], a.p32[:, :])
                kb.tt("DVE", a.pl[:, pb, :], a.p32[:, :], a.ph[:, pb, :],
                      OP.subtract)
                if pend_av is not None:
                    kb.mm(*pend_av)
                    if it == NQ:
                        kb.ts("DVE", a.invd[64:65, 0, 0:T],
                              ps["H"][64:65, 0:T], 1e-30, None, OP.max)
                        kb.recip(a.invd[64:65, 0, 0:T], a.invd[64:65, 0, 0:T])
                vcol = 65 * h
                pav = []
                for ci in range(3):
                    vt = a.vbh[:, qs + ci, vcol:vcol + 65]
                    vtl = a.vbl[:, qs + ci, vcol:vcol + 65]
                    pch = a.ph[:, pb, 128 * ci:128 * ci + 128]
                    pcl = a.pl[:, pb, 128 * ci:128 * ci + 128]
                    pav += [(vt, pch), (vtl, pch), (vt, pcl)]
                pend_av = (pav, avb[hh][0:65, 128 * qs:128 * (qs + 1)])
        kb.mm(*pend_av)
        kb.ts("DVE", a.invd[64:65, 1, 0:T], ps["G"][64:65, 0:T], 1e-30,
              None, OP.max)
        kb.recip(a.invd[64:65, 1, 0:T], a.invd[64:65, 1, 0:T])
        kb.mm([(a.ones65[64:65, :], a.invd[64:65, 0, 0:T])],
              ps["F"][0:64, 0:T])
        kb.cp("ACT", a.invd[0:64, 0, 0:T], ps["F"][0:64, 0:T])
        kb.tt("DVE", a.an32[:, 0:T], ps["H"][0:64, 0:T],
              a.invd[0:64, 0, 0:T], OP.mult)
        kb.cp("ACT", a.aoh[0:64, hp, 0:T], a.an32[:, 0:T])
        kb.tt("DVE", a.aol[0:64, hp, 0:T], a.an32[:, 0:T],
              a.aoh[0:64, hp, 0:T], OP.subtract)

    def norm1(hp):
        kb.mm([(a.ones65[64:65, :], a.invd[64:65, 1, 0:T])],
              ps["D"][0:64, 0:T])
        kb.cp("ACT", a.invd[0:64, 1, 0:T], ps["D"][0:64, 0:T])
        kb.tt("DVE", a.an32[:, 0:T], ps["G"][0:64, 0:T],
              a.invd[0:64, 1, 0:T], OP.mult)
        kb.mm([(a.pf32[0:64, 2, :], a.an32[:, 0:T])], ps["D"][:, 0:T])
        kb.cp("ACT", a.aoh[64:128, hp, 0:T], ps["D"][64:128, 0:T])
        kb.tt("DVE", a.aol[64:128, hp, 0:T], ps["D"][64:128, 0:T],
              a.aoh[64:128, hp, 0:T], OP.subtract)

    for hp in range(9):
        if hp < 8:
            bi = hp % 2
            emit_w_load_pre(kb, b, wd, bi, 0, WS_QKV + 24 * hp, 24)
            pq = [(b.wh[bi][:, kc, :], thv(kc)) for kc in range(8)]
            pq += [(b.wh[bi][:, kc, :], tlv(kc)) for kc in range(8)]
            pq += [(b.wl[bi][:, kc, :], thv(kc)) for kc in range(8)]
            kb.mm(pq, ps["A"][:, 0:T])
            pk = [(b.wh[bi][:, 8 + kc, :], thv(kc)) for kc in range(8)]
            pk += [(b.wh[bi][:, 8 + kc, :], tlv(kc)) for kc in range(8)]
            pk += [(b.wl[bi][:, 8 + kc, :], thv(kc)) for kc in range(8)]
            kb.mm(pk, ps["B"][:, 0:T])
        if hp > 0:
            attn_pair(hp - 1)
        if hp == 8:
            norm1(7)
            break
        def vgroup(tc):
            pv = [(thv(kc)[:, 128 * tc:128 * (tc + 1)], b.wh[bi][:, 16 + kc, :])
                  for kc in range(8)]
            pv += [(tlv(kc)[:, 128 * tc:128 * (tc + 1)], b.wh[bi][:, 16 + kc, :])
                   for kc in range(8)]
            pv += [(thv(kc)[:, 128 * tc:128 * (tc + 1)], b.wl[bi][:, 16 + kc, :])
                   for kc in range(8)]
            kb.mm(pv, ps["C"][:, 128 * tc:128 * tc + 128])
        if getattr(a, "_evtok", None) is not None:
            kb.wait_token("PE", a._evtok)
        for tc in range(NQ - 1):
            vgroup(tc)
        if hp > 0:
            norm1(hp - 1)
        emit_qk_pipe_s(kb, b, a, g, ps["A"], ps["E"], ps["G"], T, SL_QS,
                       b.muR)
        emit_qk_pipe_s(kb, b, a, g, ps["B"], ps["F"], ps["H"], T, SL_KS,
                       b.ge32)
        emit_qk_pipe_r(kb, b, a, g, ps["E"], ps["G"], T, False,
                       b.muR, a.qp32, a.rqk)
        emit_qk_pipe_r(kb, b, a, g, ps["F"], ps["H"], T, True,
                       b.ge32, b.rR, a.invd[0:2, 0, :])
        vgroup(NQ - 1)
        qh_, ql_ = qsel(hp)
        emit_qk_pipe_b(kb, b, a, g, ps["G"], T, a.qp32, a.rqk)
        for hh in range(2):
            emit_dup_forms(kb, a, ps["D"], ps["F"], T, hh,
                           qh_[:, hh, 0:T], ql_[:, hh, 0:T])
        emit_qk_pipe_b(kb, b, a, g, ps["H"], T, b.rR, a.invd[0:2, 0, :])
        emit_stack_form(kb, a, ps["D"], T, 0, a.khl[:, 2 * hp, 256:256 + T])
        emit_stack_form(kb, a, ps["F"], T, 1, a.khl[:, 2 * hp + 1, 256:256 + T])
        for tc in range(NQ):
            for hh in range(2):
                h = 2 * hp + hh
                vcol = 65 * h
                kb.cp("ACT" if hh == 0 else "DVE",
                      a.vbh[:, 2 + tc, vcol:vcol + 64],
                      ps["C"][:, 128 * tc + 64 * hh:128 * tc + 64 * hh + 64])
                a._evtok = kb.tt(
                    "DVE", a.vbl[:, 2 + tc, vcol:vcol + 64],
                    ps["C"][:, 128 * tc + 64 * hh:128 * tc + 64 * hh + 64],
                    a.vbh[:, 2 + tc, vcol:vcol + 64], OP.subtract)
    for oc in range(g.FC):
        bi = oc % 2
        emit_w_load_pre(kb, b, wd, bi, 0, WS_WO + 8 * oc, 8)
        pX = [ps["A"], ps["B"], ps["C"], ps["G"]][oc % 4]
        pw = [(b.wh[bi][:, kc, :], a.aoh[:, kc, 0:T]) for kc in range(8)]
        pw += [(b.wh[bi][:, kc, :], a.aol[:, kc, 0:T]) for kc in range(8)]
        pw += [(b.wl[bi][:, kc, :], a.aoh[:, kc, 0:T]) for kc in range(8)]
        kb.mm(pw, pX[:, 0:T])
        kb.tt("DVE", b.r132[:, 0:T], pX[:, 0:T], b.yh[:, oc, 0:T],
              OP.add)
        kb.tt("DVE", b.y32[:, 0:T], b.r132[:, 0:T], b.yl[:, oc, 0:T],
              OP.add)
        kb.cp("ACT", b.yh[:, oc, 0:T], b.y32[:, 0:T])
        kb.tt("DVE", b.yl[:, oc, 0:T], b.y32[:, 0:T],
              b.yh[:, oc, 0:T], OP.subtract)
        if post_oc is not None:
            post_oc(oc)


def emit_inner(kb, b, a, g, ps, wdd, lvec_in, yhd, yld):
    for li in range(g.NL):
        wd = wdd(li)
        kb.dma("SP", b.lvec[:, :], lvec_in[li, :, :])
        emit_hsv_init(kb, b, g, b.hsv)
        emit_hsv_init(kb, b, g, b.hsv2)
        nst = g.st[li + 1] if li + 1 < g.NL else 0
        for ti, (c0, c1) in enumerate(g.tiles(li)):
            T = c1 - c0
            b.cm_col = wd["cmt"][:, ti:ti + 1] if "cmt" in wd else None
            for c in range(g.FC):
                kb.dma("SP", b.yh[:, c, 0:T], yhd[:, c, c0:c1])
                kb.dma("SP", b.yl[:, c, 0:T], yld[:, c, c0:c1])

            def ffn_pre(c, T=T):
                if c < 4:
                    kb.cp("ACT", b.th[:, c, 0:T], b.yh[:, c, 0:T])
                    kb.cp("DVE", b.tl[:, c, 0:T], b.yl[:, c, 0:T])
                else:
                    kb.cp("ACT", b.th[:, c, 0:1], b.hsv2[:, 0, c:c + 1])
                    kb.cp("DVE", b.tl[:, c, 0:1], b.hsv2[:, 1, c:c + 1])
                    kb.cp("ACT", b.th[:, c, 1:T], b.yh[:, c, 0:T - 1])
                    kb.cp("DVE", b.tl[:, c, 1:T], b.yl[:, c, 0:T - 1])
                    kb.cp("DVE", b.hsv2[:, 0, c:c + 1], b.yh[:, c, T - 1:T])
                    kb.cp("DVE", b.hsv2[:, 1, c:c + 1], b.yl[:, c, T - 1:T])
                kb.mm_acc([(b.ones128[:, :], b.th[:, c, 0:T]),
                           (b.ones128[:, :], b.tl[:, c, 0:T])],
                          ps["E"][:, 0:T], start=(c == 0),
                          stop=(c == g.FC - 1))
                kb.tt("DVE", b.t32[:, 0:T], b.th[:, c, 0:T],
                      b.tl[:, c, 0:T], OP.add)
                kb.act(b.sq32[:, 0:T], b.t32[:, 0:T], AF.Square)
                kb.cp("ACT", b.sqh[:, 0:T], b.sq32[:, 0:T])
                kb.tt("DVE", b.sql[:, 0:T], b.sq32[:, 0:T], b.sqh[:, 0:T],
                      OP.subtract)
                kb.mm_acc([(b.ones128[:, :], b.sqh[:, 0:T]),
                           (b.ones128[:, :], b.sql[:, 0:T])],
                          ps["F"][:, 0:T], start=(c == 0),
                          stop=(c == g.FC - 1))
            emit_attn_sublayer(kb, b, a, g, ps, li, wd, c0, c1,
                               post_oc=(ffn_pre if STATS_FOLD else None))
            emit_ffn_sublayer(kb, b, g, ps, li, wd, c0, c1,
                              pre_done=STATS_FOLD)
            b.cm_col = None
            if c0 <= nst - 1 < c1:
                j = nst - 1 - c0
                for c in range(g.FC):
                    kb.cp("DVE", b.hsnap[:, 0, c:c + 1], b.yh[:, c, j:j + 1])
                    kb.cp("DVE", b.hsnap[:, 1, c:c + 1], b.yl[:, c, j:j + 1])
            kb.dma("POOL", yhd[:, :, c0:c1], b.yh[:, :, 0:T])
            kb.dma("POOL", yld[:, :, c0:c1], b.yl[:, :, 0:T])


ALIBI_CUT = 112.0

def alibi_slopes():
    return np.array([2.0 ** (-8.0 / n) for n in range(16, 0, -1)], np.float64)


class OPlan:

    def __init__(self, g, S=4096, baseL=-512, baseR=1024):
        self.g = g
        self.S = S
        self.NKC = S // 128
        self.NQT = g.W // 512
        self.bL = baseL // 128
        self.bR = baseR // 128
        self.baseL = baseL
        self.baseR = baseR
        sl = alibi_slopes()
        self.slopes = sl
        self.cut = [ALIBI_CUT / s for s in sl]
        self.entries = []
        self.NE = 0
        self.ebidx = {}
        for h in range(16):
            eh = []
            for qt in range(self.NQT):
                lst = []
                for kc in range(self.NKC):
                    d = kc - 4 * qt
                    if self.bL <= d <= self.bL + 3:
                        cls, z = 2, d - self.bL
                    elif self.bR <= d <= self.bR + 3:
                        cls, z = 4, d - self.bR
                    elif d < self.bL:
                        cls, z = 0, 0
                    elif d < self.bR:
                        cls, z = 3, 0
                    else:
                        cls, z = 1, 0
                    keep = False
                    for base in (self.baseL, self.baseR):
                        i0 = base + 512 * qt
                        dist = max(i0 - (128 * kc + 127), 128 * kc - (i0 + 511), 0)
                        if dist <= ALIBI_CUT / sl[h]:
                            keep = True
                    if keep:
                        if cls in (0, 1, 3):
                            self.ebidx[(h, qt, kc)] = self.NE
                            self.NE += 1
                        lst.append((kc, cls, z))
                eh.append(lst)
            self.entries.append(eh)


def build_outer_data(plan, base):
    sl = plan.slopes
    NE = plan.NE
    eb = np.zeros((128, max(NE, 1)), np.float32)
    mz = np.zeros((16, 2, 4, 128, 512), np.float32)
    gt = np.zeros((16, 5, 65, 512), np.float32)
    jr = np.arange(128, dtype=np.float64)
    av = np.arange(512, dtype=np.float64)
    C2 = 576.0
    own = {2: base == plan.baseL, 4: base == plan.baseR}
    for h in range(16):
        s = sl[h]
        g_past = np.exp(-s * (av + 64.0))
        g_fut = np.exp(-s * (C2 - av))
        ones = np.ones(512)
        grow = {0: g_past, 1: g_fut}
        grow[3] = g_fut if base == plan.baseL else g_past
        grow[2] = ones if own[2] else g_past
        grow[4] = ones if own[4] else g_fut
        for cls in range(5):
            gt[h, cls, 0:65, :] = grow[cls][None, :]
        for zi, cls in ((0, 2), (1, 4)):
            for z in range(4):
                if own[cls]:
                    dj = (128 * z + jr)[:, None] - av[None, :]
                    mz[h, zi, z] = np.exp(-s * np.abs(dj)).astype(np.float32)
                else:
                    if cls == 2:
                        off = 128.0 * (z + plan.bL - plan.bR) + jr
                        mz[h, zi, z] = np.exp(s * (off + 64.0))[:, None].astype(np.float32)
                    else:
                        off = 128.0 * (z + plan.bR - plan.bL) + jr
                        mz[h, zi, z] = np.exp(s * (C2 - off))[:, None].astype(np.float32)
        for qt in range(plan.NQT):
            i0 = base + 512 * qt
            for (kc, cls, z) in plan.entries[h][qt]:
                if cls in (2, 4):
                    continue
                e = plan.ebidx[(h, qt, kc)]
                j = 128.0 * kc + jr
                if (cls == 0) or (cls == 3 and base == plan.baseR):
                    eb[:, e] = np.exp(s * (j - i0 + 64.0))
                else:
                    eb[:, e] = np.exp(s * (i0 + C2 - j))
    return eb, mz, gt


def alloc_outer_bufs(kb, g, plan, gl=None):
    o = Bufs()
    W = g.W
    o.xth = kb.sb("oxth", [128, 8, 512], BF16)
    o.xtl = kb.sb("oxtl", [128, 8, 512], BF16)
    o.qhh = kb.sb("oqhh", [128, 2, W], BF16)
    o.qll = kb.sb("oqll", [128, 2, W], BF16)
    o.khl = kb.sb("okhl", [128, 2, plan.S], BF16)
    o.vbh = kb.sb("ovbh", [128, plan.S // 128, 130], BF16)
    o.vbl = kb.sb("ovbl", [128, plan.S // 128, 130], BF16)
    o.aoph = kb.sb("oaoph", [128, W], BF16)
    o.aopl = kb.sb("oaopl", [128, W], BF16)
    o.p32 = gl.sq32 if gl is not None else kb.sb("op32", [128, 512], F32)
    o.ph = kb.sb("oph", [128, 512], BF16)
    o.pl = kb.sb("opl", [128, 512], BF16)
    o.u32 = gl.u32 if gl is not None else kb.sb("ou32", [128, 512], F32)
    o.mz = (gl.x32 if gl is not None else
            kb.sb("omz", [128, 8, 512], F32))
    o.gt = kb.sb("ogt", [65, 5, 512], F32)
    o.ebt = kb.sb("oebt", [128, max(plan.NE, 1)], F32)
    o.att32 = kb.sb("oatt32", [65, 512], F32)
    o.invd = kb.sb("oinvd", [65, 512], F32)
    o.invdR = kb.sb("oinvdR", [64, 512], F32)
    o.ones65 = kb.sb("oones65", [65, 64], F32)
    o.pbf = kb.sb("opbf", [128, 6, 128], BF16)
    o.pf32 = kb.sb("opf32", [128, 4, 128], F32)
    o.qhb = kb.sb("oqhb", [128, 512], BF16)
    o.qlb = kb.sb("oqlb", [128, 512], BF16)
    o.an1 = kb.sb("oan1", [64, 512], F32)
    o.xth2 = gl.aoh if gl is not None else kb.sb("oxth2", [128, 8, 512], BF16)
    o.xtl2 = gl.aol if gl is not None else kb.sb("oxtl2", [128, 8, 512], BF16)
    o.qhb2 = gl.oh if gl is not None else kb.sb("oqhb2", [128, 512], BF16)
    o.qlb2 = gl.ol if gl is not None else kb.sb("oqlb2", [128, 512], BF16)
    o.e32b = gl.muR if gl is not None else kb.sb("oe32b", [128, 512], F32)
    o.u32b = gl.m2 if gl is not None else kb.sb("ou32b", [128, 512], F32)
    o.ph2 = gl.sqh if gl is not None else kb.sb("oph2", [128, 512], BF16)
    o.pl2 = gl.sql if gl is not None else kb.sb("opl2", [128, 512], BF16)
    return o


def emit_outer_attn(kb, b, o, g, plan, pso, wq, wk, wv, xfh, xfl, xsh, xsl,
                    mzin, gtin, ebin, aodh, aodl, cf32o=None, pbfo=None,
                    pf32o=None, dbg=None, bg=None):
    if bg is None:
        bg = lambda n=1: None
    """Outer ALiBi attention. x' full (keys) in xfh/xfl [128,8,S] bf16 HBM;
    x' span (queries) in xsh/xsl [128,8,W]. Writes per-pair attention output
    (normalized, hi/lo) to aodh/aodl [8, 128, W] HBM scratch."""
    W = g.W
    NQT = plan.NQT
    NKT = plan.S // 512
    kb.dma("SP", o.ebt[:, :], ebin)
    kb.dma("SP", o.ones65[64:65, :], cf32o[64:65, 128:192])
    kb.dma("SP", o.pbf[:, :, :], pbfo)
    kb.dma("SP", o.pf32[:, :, :], pf32o)
    kb.memset("DVE", o.vbh[:, :, :], 0.0)
    kb.memset("DVE", o.vbl[:, :, :], 0.0)
    for hh in range(2):
        kb.memset("DVE", o.vbh[:, :, 65 * hh + 64:65 * hh + 65], 1.0)
    for hp in range(8):
        bi = hp % 2
        emit_w_load(kb, b, g, wq, 128 * hp, 128, bi, slot0=0)
        emit_w_load(kb, b, g, wk, 128 * hp, 128, bi, slot0=8)
        emit_w_load(kb, b, g, wv, 128 * hp, 128, bi, slot0=16)
        xb = [(o.xth, o.xtl), (o.xth2, o.xtl2)]
        qb = [(o.qhb, o.qlb), (o.qhb2, o.qlb2)]
        for qt in range(NQT):
            xh_, xl_ = xb[qt % 2]
            qh_, ql_ = qb[qt % 2]
            sc_ = pso["SC"][:, 512 * (qt % 2):512 * (qt % 2) + 512]
            kb.dma("SP", xh_[:, :, :], xsh[:, :, 512 * qt:512 * (qt + 1)])
            kb.dma("POOL", xl_[:, :, :], xsl[:, :, 512 * qt:512 * (qt + 1)])
            pq = [(b.wh[bi][:, kc, :], xh_[:, kc, :]) for kc in range(8)]
            pq += [(b.wh[bi][:, kc, :], xl_[:, kc, :]) for kc in range(8)]
            pq += [(b.wl[bi][:, kc, :], xh_[:, kc, :]) for kc in range(8)]
            kb.mm(pq, sc_)
            sl_ = slice(512 * qt, 512 * (qt + 1))
            kb.cp("DVE", qh_[:, :], sc_)
            kb.tt("DVE", ql_[:, :], sc_, qh_[:, :], OP.subtract)
            for hh in range(2):
                kb.mm([(o.pbf[:, hh, :], qh_[:, :])], pso["BC"][:, 0:512])
                kb.mm([(o.pbf[:, hh, :], ql_[:, :])], pso["V1"][:, 0:512])
                kb.cp("DVE", o.qhh[:, hh, sl_], pso["BC"][:, 0:512])
                kb.cp("DVE", o.qll[:, hh, sl_], pso["V1"][:, 0:512])
            bg(4)
        for kt in range(NKT):
            xh_, xl_ = xb[kt % 2]
            qh_, ql_ = qb[kt % 2]
            sc_ = pso["SC"][:, 512 * (kt % 2):512 * (kt % 2) + 512]
            u32_ = [o.u32, o.u32b][kt % 2]
            vb_ = pso["V0"] if kt % 2 == 0 else pso["V2"]
            kb.dma("SP", xh_[:, :, :], xfh[:, :, 512 * kt:512 * (kt + 1)])
            kb.dma("POOL", xl_[:, :, :], xfl[:, :, 512 * kt:512 * (kt + 1)])
            pk = [(b.wh[bi][:, 8 + kc, :], xh_[:, kc, :]) for kc in range(8)]
            pk += [(b.wh[bi][:, 8 + kc, :], xl_[:, kc, :]) for kc in range(8)]
            pk += [(b.wl[bi][:, 8 + kc, :], xh_[:, kc, :]) for kc in range(8)]
            kb.mm(pk, sc_)
            kb.act(u32_[:, 0:512], sc_, AF.Copy, scale=0.125)
            sl_ = slice(512 * kt, 512 * (kt + 1))
            kb.cp("DVE", qh_[:, :], u32_[:, 0:512])
            kb.tt("DVE", ql_[:, :], u32_[:, 0:512], qh_[:, :], OP.subtract)
            kb.mm([(o.pbf[:, 2, :], qh_[:, :]),
                   (o.pbf[:, 4, :], ql_[:, :])], pso["BC"][:, 0:512])
            kb.mm([(o.pbf[:, 3, :], qh_[:, :]),
                   (o.pbf[:, 5, :], ql_[:, :])], pso["V1"][:, 0:512])
            kb.cp("DVE", o.khl[:, 0, sl_], pso["BC"][:, 0:512])
            kb.cp("DVE", o.khl[:, 1, sl_], pso["V1"][:, 0:512])
            if getattr(o, "_evtok", None) is not None:
                kb.wait_token("PE", o._evtok)
            tkv = None
            for tc in range(4):
                pv = [(xh_[:, kc, 128 * tc:128 * (tc + 1)], b.wh[bi][:, 16 + kc, :])
                      for kc in range(8)]
                pv += [(xl_[:, kc, 128 * tc:128 * (tc + 1)], b.wh[bi][:, 16 + kc, :])
                       for kc in range(8)]
                pv += [(xh_[:, kc, 128 * tc:128 * (tc + 1)], b.wl[bi][:, 16 + kc, :])
                       for kc in range(8)]
                tkv = kb.mm(pv, vb_[:, 128 * tc:128 * tc + 128])
            kb.wait_token("DVE", tkv)
            for tc in range(4):
                for hh in range(2):
                    vc = 65 * hh
                    kb.cp("DVE", o.vbh[:, 4 * kt + tc, vc:vc + 64],
                          vb_[:, 128 * tc + 64 * hh:128 * tc + 64 * hh + 64])
                    o._evtok = kb.tt(
                        "DVE", o.vbl[:, 4 * kt + tc, vc:vc + 64],
                        vb_[:, 128 * tc + 64 * hh:128 * tc + 64 * hh + 64],
                        o.vbh[:, 4 * kt + tc, vc:vc + 64], OP.subtract)
            bg(4)
        for hh in range(2):
            h = 2 * hp + hh
            kb.dma("POOL", o.mz[:, :, :], mzin[h, :, :, :, :].rearrange("a b p q -> p (a b) q"))
            kb.dma("ACT", o.gt[:, :, :], gtin[h, :, :, :].rearrange("c p q -> p c q"))
            for qt in range(NQT):
                ents = plan.entries[h][qt]
                if not ents:
                    continue
                cls_last = {}
                cls_first = {}
                for (kc, cls, z) in ents:
                    cls_last[cls] = kc
                    if cls not in cls_first:
                        cls_first[cls] = kc
                qsl_ = slice(512 * qt, 512 * (qt + 1))
                pend = None
                for i, (kc, cls, z) in enumerate(ents):
                    half = i % 2
                    ph_, pl_ = (o.ph, o.pl) if half == 0 else (o.ph2, o.pl2)
                    e32 = [o.p32, o.e32b][half][:, 0:512]
                    u32_ = [o.u32, o.u32b][half]
                    kb.mm([(o.khl[:, hh, 128 * kc:128 * (kc + 1)], o.qhh[:, hh, qsl_]),
                           (o.khl[:, hh, 128 * kc:128 * (kc + 1)], o.qll[:, hh, qsl_])],
                          pso["SC"][:, 512 * half:512 * (half + 1)])
                    kb.act(e32, pso["SC"][:, 512 * half:512 * (half + 1)],
                           AF.Exp)
                    if cls in (2, 4):
                        zi = 0 if cls == 2 else 4
                        kb.tt("DVE", u32_[:, :], e32, o.mz[:, zi + z, :], OP.mult)
                        kb.cp("DVE", ph_[:, :], u32_[:, :])
                        kb.tt("DVE", pl_[:, :], u32_[:, :], ph_[:, :], OP.subtract)
                    else:
                        e = plan.ebidx[(h, qt, kc)]
                        kb.ts("DVE", ph_[:, :], e32, o.ebt[:, e:e + 1], None, OP.mult)
                        kb.stt(pl_[:, :], e32, o.ebt[:, e:e + 1], ph_[:, :],
                               OP.mult, OP.subtract)
                    if pend is not None:
                        kb.mm_acc(*pend[0], **pend[1])
                    bg(1)
                    vc = 65 * hh
                    vt_h = o.vbh[:, kc, vc:vc + 65]
                    vt_l = o.vbl[:, kc, vc:vc + 65]
                    pend = (([(vt_h, ph_[:, :]), (vt_l, ph_[:, :]),
                              (vt_h, pl_[:, :])],
                             pso["V%d" % cls][0:65, 0:512]),
                            dict(start=(cls_first[cls] == kc),
                                 stop=(cls_last[cls] == kc)))
                if pend is not None:
                    kb.mm_acc(*pend[0], **pend[1])
                pres = sorted(cls_first.keys())
                first = True
                for cls in pres:
                    if first:
                        kb.tt("DVE", o.att32[:, :], pso["V%d" % cls][0:65, 0:512],
                              o.gt[:, cls, :], OP.mult)
                        first = False
                    else:
                        kb.tt("DVE", o.u32[0:65, :], pso["V%d" % cls][0:65, 0:512],
                              o.gt[:, cls, :], OP.mult)
                        kb.tt("DVE", o.att32[:, :], o.att32[:, :], o.u32[0:65, :], OP.add)
                if dbg is not None:
                    kb.dma("SP", dbg[h, qt, :, :], o.att32[:, :])
                kb.ts("DVE", o.invd[64:65, :], o.att32[64:65, :], 1e-30, None,
                      OP.max)
                kb.recip(o.invd[64:65, :], o.invd[64:65, :])
                kb.mm([(o.ones65[64:65, :], o.invd[64:65, :])],
                      pso["BC"][0:64, 0:512])
                kb.cp("ACT", o.invdR[:, :], pso["BC"][0:64, 0:512])
                kb.tt("DVE", o.an1[:, :], o.att32[0:64, :], o.invdR[:, :],
                      OP.mult)
                if hh == 0:
                    kb.cp("DVE", o.aoph[0:64, qsl_], o.an1[:, :])
                    kb.tt("DVE", o.aopl[0:64, qsl_], o.an1[:, :],
                          o.aoph[0:64, qsl_], OP.subtract)
                else:
                    kb.mm([(o.pf32[0:64, 2, :], o.an1[:, :])],
                          pso["BC"][:, 0:512])
                    kb.cp("DVE", o.aoph[64:128, qsl_], pso["BC"][64:128, 0:512])
                    kb.tt("DVE", o.aopl[64:128, qsl_], pso["BC"][64:128, 0:512],
                          o.aoph[64:128, qsl_], OP.subtract)
                bg(2)
        kb.dma("ACT", aodh[hp, :, :], o.aoph[:, :])
        kb.dma("ACT", aodl[hp, :, :], o.aopl[:, :])


OV_G = 0
OV_B = 8
OV_EPS = 16


def alloc_glue_bufs(kb, g):
    gl = Bufs()
    gl.x32 = kb.sb("glx32", [128, 8, 512], F32)
    gl.sq32 = kb.sb("glsq", [128, 512], F32)
    gl.sqh = kb.sb("glsqh", [128, 512], BF16)
    gl.sql = kb.sb("glsql", [128, 512], BF16)
    gl.muR = kb.sb("glmu", [128, 512], F32)
    gl.m2 = kb.sb("glm2", [128, 512], F32)
    gl.rR = kb.sb("glrR", [128, 512], F32)
    gl.u32 = kb.sb("glu32", [128, 512], F32)
    gl.v32 = kb.sb("glv32", [128, 512], F32)
    gl.oh = kb.sb("gloh", [128, 512], BF16)
    gl.ol = kb.sb("glol", [128, 512], BF16)
    gl.ones128 = kb.sb("glones", [128, 128], BF16)
    gl.ovec = kb.sb("glovec", [128, 32], F32)
    gl.aoh = kb.sb("glaoh", [128, 8, 512], BF16)
    gl.aol = kb.sb("glaol", [128, 8, 512], BF16)
    return gl


def _glue_stats(kb, gl, ps, T, lnln):
    passes = []
    for c in range(8):
        kb.cp("DVE", gl.sqh[:, 0:T], gl.x32[:, c, 0:T])
        kb.tt("DVE", gl.sql[:, 0:T], gl.x32[:, c, 0:T], gl.sqh[:, 0:T], OP.subtract)
        kb.mm_acc([(gl.ones128[:, :], gl.sqh[:, 0:T]),
                   (gl.ones128[:, :], gl.sql[:, 0:T])],
                  ps["E"][:, 0:T], start=(c == 0), stop=(c == 7))
        kb.act(gl.sq32[:, 0:T], gl.x32[:, c, 0:T], AF.Square)
        kb.cp("DVE", gl.sqh[:, 0:T], gl.sq32[:, 0:T])
        kb.tt("DVE", gl.sql[:, 0:T], gl.sq32[:, 0:T], gl.sqh[:, 0:T], OP.subtract)
        kb.mm_acc([(gl.ones128[:, :], gl.sqh[:, 0:T]),
                   (gl.ones128[:, :], gl.sql[:, 0:T])],
                  ps["F"][:, 0:T], start=(c == 0), stop=(c == 7))
    kb.ts("DVE", gl.muR[:, 0:T], ps["E"][:, 0:T], INV_D, None, OP.mult)
    kb.tt("DVE", gl.m2[:, 0:T], gl.muR[:, 0:T], gl.muR[:, 0:T], OP.mult)
    kb.stt(gl.u32[:, 0:T], ps["F"][:, 0:T], INV_D, gl.m2[:, 0:T],
           OP.mult, OP.subtract)
    if not lnln:
        kb.act(gl.m2[:, 0:T], gl.u32[:, 0:T], AF.Sqrt,
               bias=gl.ovec[:, OV_EPS:OV_EPS + 1])
        kb.recip(gl.rR[:, 0:T], gl.m2[:, 0:T])
    else:
        kb.ts("DVE", gl.m2[:, 0:T], gl.u32[:, 0:T], 1e-5, None, OP.add)
        kb.recip(gl.v32[:, 0:T], gl.m2[:, 0:T])
        kb.stt(gl.v32[:, 0:T], gl.u32[:, 0:T], 1.0, gl.v32[:, 0:T],
               OP.mult, OP.mult)
        kb.ts("DVE", gl.v32[:, 0:T], gl.v32[:, 0:T], 1e-5, None, OP.add)
        kb.tt("DVE", gl.m2[:, 0:T], gl.m2[:, 0:T], gl.v32[:, 0:T], OP.mult)
        kb.act(gl.m2[:, 0:T], gl.m2[:, 0:T], AF.Sqrt)
        kb.recip(gl.rR[:, 0:T], gl.m2[:, 0:T])


def emit_xprime(kb, gl, g, ps, xin, N, outh, outl, affine):
    nt = N // 512
    for t in range(nt):
        sl_ = slice(512 * t, 512 * (t + 1))
        kb.dma("SP", gl.x32[:, :, :], xin[:, :, sl_])
        _glue_stats(kb, gl, ps, 512, lnln=True)
        for c in range(8):
            kb.tt("DVE", gl.u32[:, :], gl.x32[:, c, :], gl.muR[:, :], OP.subtract)
            kb.tt("DVE", gl.u32[:, :], gl.u32[:, :], gl.rR[:, :], OP.mult)
            if affine:
                kb.ts("DVE", gl.u32[:, :], gl.u32[:, :],
                      gl.ovec[:, OV_G + c:OV_G + c + 1],
                      gl.ovec[:, OV_B + c:OV_B + c + 1], OP.mult, OP.add)
            kb.cp("DVE", gl.oh[:, :], gl.u32[:, :])
            kb.tt("DVE", gl.ol[:, :], gl.u32[:, :], gl.oh[:, :], OP.subtract)
            kb.dma("SP", outh[:, c, sl_], gl.oh[:, :])
            kb.dma("SP", outl[:, c, sl_], gl.ol[:, :])


def emit_wo_x2_y0(kb, bw, gl, g, ps, wo_dram, aodh, aodl, xcore, x2d, yhd, yld):
    nt = g.W // 512
    ao_h = gl.aoh
    ao_l = gl.aol
    for t in range(nt):
        sl_ = slice(512 * t, 512 * (t + 1))
        kb.dma("SP", ao_h[:, :, :], aodh[:, :, sl_].rearrange("a p q -> p a q"))
        kb.dma("SP", ao_l[:, :, :], aodl[:, :, sl_].rearrange("a p q -> p a q"))
        for oc in range(8):
            bi = oc % 2
            emit_w_load(kb, bw, g, wo_dram, 128 * oc, 128, bi, slot0=0)
            pw = [(bw.wh[bi][:, kc, :], ao_h[:, kc, :]) for kc in range(8)]
            pw += [(bw.wh[bi][:, kc, :], ao_l[:, kc, :]) for kc in range(8)]
            pw += [(bw.wl[bi][:, kc, :], ao_h[:, kc, :]) for kc in range(8)]
            kb.mm(pw, ps["A"][:, 0:512])
            kb.cp("ACT", gl.x32[:, oc, :], ps["A"][:, 0:512])
        _glue_stats(kb, gl, ps, 512, lnln=False)
        for c in range(8):
            kb.tt("DVE", gl.u32[:, :], gl.x32[:, c, :], gl.muR[:, :], OP.subtract)
            kb.tt("DVE", gl.u32[:, :], gl.u32[:, :], gl.rR[:, :], OP.mult)
            kb.dma("SP", gl.v32[:, :], xcore[:, c, sl_])
            kb.tt("DVE", gl.x32[:, c, :], gl.u32[:, :], gl.v32[:, :], OP.add)
            kb.dma("SP", x2d[:, c, sl_], gl.x32[:, c, :])
        _glue_stats(kb, gl, ps, 512, lnln=False)
        for c in range(8):
            kb.tt("DVE", gl.u32[:, :], gl.x32[:, c, :], gl.muR[:, :], OP.subtract)
            kb.tt("DVE", gl.u32[:, :], gl.u32[:, :], gl.rR[:, :], OP.mult)
            kb.cp("DVE", gl.oh[:, :], gl.u32[:, :])
            kb.tt("DVE", gl.ol[:, :], gl.u32[:, :], gl.oh[:, :], OP.subtract)
            kb.dma("SP", yhd[:, c, sl_], gl.oh[:, :])
            kb.dma("SP", yld[:, c, sl_], gl.ol[:, :])


def emit_final(kb, gl, g, ps, yhd, yld, x2d, outd):
    nt = g.W // 512
    for t in range(nt):
        sl_ = slice(512 * t, 512 * (t + 1))
        for c in range(8):
            kb.dma("SP", gl.oh[:, :], yhd[:, c, sl_])
            kb.dma("SP", gl.ol[:, :], yld[:, c, sl_])
            kb.tt("DVE", gl.x32[:, c, :], gl.oh[:, :], gl.ol[:, :], OP.add)
        _glue_stats(kb, gl, ps, 512, lnln=False)
        for c in range(8):
            kb.tt("DVE", gl.u32[:, :], gl.x32[:, c, :], gl.muR[:, :], OP.subtract)
            kb.tt("DVE", gl.u32[:, :], gl.u32[:, :], gl.rR[:, :], OP.mult)
            kb.dma("SP", gl.v32[:, :], x2d[:, c, sl_])
            kb.tt("DVE", gl.u32[:, :], gl.u32[:, :], gl.v32[:, :], OP.add)
            kb.dma("SP", outd[:, c, sl_], gl.u32[:, :])


def build_full(g, plan, NL):
    S = plan.S
    W = g.W
    nc = bass.Bass("TRN2", target_bir_lowering=False)
    ap = lambda t: t.ap()
    xfull = nc.dram_tensor("xfull", [128, 8, S], F32, kind="ExternalInput").ap()
    xcore = nc.dram_tensor("xcore", [128, 8, W], F32, kind="ExternalInput").ap()
    owq = nc.dram_tensor("owq", [1024, 1024], F32, kind="ExternalInput").ap()
    owk = nc.dram_tensor("owk", [1024, 1024], F32, kind="ExternalInput").ap()
    owv = nc.dram_tensor("owv", [1024, 1024], F32, kind="ExternalInput").ap()
    owo = nc.dram_tensor("owo", [1024, 1024], F32, kind="ExternalInput").ap()
    ovin = nc.dram_tensor("ovin", [128, 32], F32, kind="ExternalInput").ap()
    fqkv = nc.dram_tensor("fqkv", [NL, 1024, 3072], F32, kind="ExternalInput").ap()
    fwo = nc.dram_tensor("fwo", [NL, 1024, 1024], F32, kind="ExternalInput").ap()
    fw1 = nc.dram_tensor("fw1", [NL, 1024, 5460], F32, kind="ExternalInput").ap()
    fw2 = nc.dram_tensor("fw2", [NL, 2730, 1024], F32, kind="ExternalInput").ap()
    lvin = nc.dram_tensor("lvin", [NL, 128, SL_NV], F32, kind="ExternalInput").ap()
    rott = nc.dram_tensor("rott_in", [128, 4, W], F32, kind="ExternalInput").ap()
    mskin = nc.dram_tensor("msk_in", [128, W // 128, 3, 128], BF16, kind="ExternalInput").ap()
    cbf = nc.dram_tensor("cbf_in", [128, 130], BF16, kind="ExternalInput").ap()
    cf32 = nc.dram_tensor("cf32_in", [128, 192], F32, kind="ExternalInput").ap()
    pbfi = nc.dram_tensor("pbf_in", [128, 6, 128], BF16, kind="ExternalInput").ap()
    pf32i = nc.dram_tensor("pf32_in", [128, 4, 128], F32, kind="ExternalInput").ap()
    mzin = nc.dram_tensor("mzin", [16, 2, 4, 128, 512], F32, kind="ExternalInput").ap()
    gtin = nc.dram_tensor("gtin", [16, 5, 65, 512], F32, kind="ExternalInput").ap()
    ebin = nc.dram_tensor("ebin", [128, max(plan.NE, 1)], F32, kind="ExternalInput").ap()
    outd = nc.dram_tensor("outd", [128, 8, W], F32, kind="ExternalOutput").ap()
    xph = nc.dram_tensor("xph", [128, 8, S], BF16).ap()
    xpl = nc.dram_tensor("xpl", [128, 8, S], BF16).ap()
    xsh = nc.dram_tensor("xsh", [128, 8, W], BF16).ap()
    xsl = nc.dram_tensor("xsl", [128, 8, W], BF16).ap()
    aodh = nc.dram_tensor("aodh", [8, 128, W], BF16).ap()
    aodl = nc.dram_tensor("aodl", [8, 128, W], BF16).ap()
    x2d = nc.dram_tensor("x2d", [128, 8, W], F32).ap()
    yhd = nc.dram_tensor("yhd", [128, 8, W], BF16).ap()
    yld = nc.dram_tensor("yld", [128, 8, W], BF16).ap()
    whd = nc.dram_tensor("whd", [NL, 128, WS_N, 128], BF16).ap()
    wld = nc.dram_tensor("wld", [NL, 128, WS_N, 128], BF16).ap()

    with contextlib.ExitStack() as gstk:
        kb = KBI(nc, gstk)
        bw = alloc_wbufs(kb)
        with contextlib.ExitStack() as stk2:
            kb.st = stk2
            gl = alloc_glue_bufs(kb, g)
            with contextlib.ExitStack() as stkx:
                kb.st = stkx
                pstmp = {"E": kb.ps("psxE", [128, 512], F32),
                         "F": kb.ps("psxF", [128, 512], F32)}
                with kb.block("xprime"):
                    kb.dma("SP", gl.ovec[:, :], ovin)
                    kb.memset("POOL", gl.ones128[:, :], 1.0)
                    emit_xprime(kb, gl, g, pstmp, xfull, S, xph, xpl, True)
                    emit_xprime(kb, gl, g, pstmp, xcore, W, xsh, xsl, True)
                    kb.drain_dmas()
                kb.st = stk2
            with contextlib.ExitStack() as stk3:
                kb.st = stk3
                o = alloc_outer_bufs(kb, g, plan, gl=gl)
                sp = alloc_wsplit_bufs(kb)
                thunks = []
                for li in range(NL):
                    thunks += ws_layer_thunks(kb, sp, g, fqkv[li], fwo[li],
                                              fw1[li], fw2[li], whd[li],
                                              wld[li])
                tpos = [0]

                def bg(n=1):
                    j = tpos[0]
                    e = min(j + n, len(thunks))
                    for k2 in range(j, e):
                        thunks[k2]()
                    tpos[0] = e
                pso = {"SC": kb.ps("psSC", [128, 1024], F32),
                       "BC": kb.ps("psBC", [128, 512], F32)}
                for i in range(5):
                    pso["V%d" % i] = kb.ps("psV%d" % i, [128, 512], F32)
                with kb.block("oattn"):
                    kb.memset("DVE", sp.sh[:, :, :, :], 0.0)
                    kb.memset("DVE", sp.sl[:, :, :, :], 0.0)
                    emit_outer_attn(kb, bw, o, g, plan, pso, owq, owk, owv,
                                    xph, xpl, xsh, xsl, mzin, gtin, ebin,
                                    aodh, aodl, cf32o=cf32, pbfo=pbfi,
                                    pf32o=pf32i, bg=bg)
                    bg(len(thunks))
                    kb.drain_dmas()
                kb.st = stk2
            pstmp2 = {"A": kb.ps("pswA", [128, 512], F32),
                      "E": kb.ps("pswE", [128, 512], F32),
                      "F": kb.ps("pswF", [128, 512], F32)}
            with kb.block("wo"):
                emit_wo_x2_y0(kb, bw, gl, g, pstmp2, owo, aodh, aodl, xcore,
                              x2d, yhd, yld)
                kb.drain_dmas()
            kb.st = gstk
        with contextlib.ExitStack() as stk4:
            kb.st = stk4
            b = alloc_inner_bufs(kb, g, bw)
            a = alloc_attn_bufs(kb, g)
            ps = {k: kb.ps(f"ps{k}", [128, 512], F32) for k in
                  ["A", "B", "C", "D", "E", "F", "G", "H"]}
            with kb.block("iinit"):
                kb.memset("POOL", b.ones128[:, :], 1.0)
                kb.memset("DVE", b.hsnap[:, :, :], 0.0)
                emit_attn_consts(kb, b, a, g, cbf, cf32, pbfi, pf32i)
            with kb.block("inner"):
                def wdd(li):
                    return {"whd": whd[li], "wld": wld[li],
                            "rott": rott, "mskin": mskin}
                emit_inner(kb, b, a, g, ps, wdd, lvin, yhd, yld)
                kb.drain_dmas()
            kb.st = gstk
        with contextlib.ExitStack() as stk5:
            kb.st = stk5
            gl2 = alloc_glue_bufs(kb, g)
            psf = {"E": kb.ps("psfE", [128, 512], F32),
                   "F": kb.ps("psfF", [128, 512], F32)}
            with kb.block("final"):
                kb.dma("SP", gl2.ovec[:, :], ovin)
                kb.memset("POOL", gl2.ones128[:, :], 1.0)
                emit_final(kb, gl2, g, psf, yhd, yld, x2d, outd)
                kb.drain_dmas()
            kb.st = gstk
    return nc


def host_masks(base, ngc):
    msk = np.zeros((128, ngc, 3, 128), BF)
    NEG = np.float32(-1e38)
    kr = np.arange(128)[:, None]
    qi = np.arange(128)[None, :]
    for gc in range(ngc):
        qg = base + 128 * gc
        va = (kr >= qi) & (qg - 256 + kr >= 0)
        vb = np.broadcast_to(qg - 128 + kr >= 0, (128, 128))
        vc = np.broadcast_to(kr <= qi, (128, 128))
        msk[:, gc, 0, :] = np.where(va, 0, NEG).astype(BF)
        msk[:, gc, 1, :] = np.where(vb, 0, NEG).astype(BF)
        msk[:, gc, 2, :] = np.where(vc, 0, NEG).astype(BF)
    return msk


def host_rotary(n, positions):
    half = np.arange(0, 64, 2, dtype=np.float64)
    inv_freq = 1.0 / 10000.0 ** (half / 64.0)
    t = positions.astype(np.float64)
    fr = t[:, None] * inv_freq[None, :]
    fr = np.concatenate([fr, fr], -1)
    cos, sin = np.cos(fr), np.sin(fr)
    bsc = (half + 0.4 * 64) / (1.4 * 64)
    power = (t - n // 2) / (256 // 2)
    sc = bsc[None, :] ** power[:, None]
    sc = np.concatenate([sc, sc], -1)
    return cos, sin, sc


def host_rott(n, positions):
    cos, sin, sc = host_rotary(n, positions)
    inv = 1.0 / sc
    T = len(positions)
    tab = np.zeros((128, 4, T), np.float32)
    for w, (c2, s2) in enumerate([(cos * sc, sin * sc), (cos * inv, sin * inv)]):
        tab[:, 2 * w + 0, :] = np.tile(c2.T.astype(np.float32), (2, 1))
        tab[:, 2 * w + 1, :] = np.tile(s2.T.astype(np.float32), (2, 1))
    return tab


def host_consts():
    cbf = np.zeros((128, 130), BF)
    cbf[:, 0:128] = np.eye(128, dtype=np.float32).astype(BF)
    cbf[0:64, 128] = 1
    cbf[64:128, 129] = 1
    cf32 = np.zeros((128, 192), np.float32)
    cf32[0, 0:64] = 1.0
    cf32[1, 64:128] = 1.0
    cf32[0, 128:192] = 1.0
    cf32[64, 128:192] = 1.0
    pbf = np.zeros((128, 6, 128), np.float32)
    for h in range(2):
        for m in range(128):
            pbf[64 * h + m % 64, h, m] = 1.0
        for m in range(64):
            pbf[64 * h + m, 2 + h, m] = 1.0
        for m in range(64, 128):
            pbf[64 * h + m - 64, 4 + h, m] = 1.0
    pbf = pbf.astype(BF)
    pf32 = np.zeros((128, 4, 128), np.float32)
    for m in range(128):
        base = (m // 64) * 64
        d = m % 64
        src = base + (d + 32 if d < 32 else d - 32)
        pf32[src, 0, m] = -1.0 if d < 32 else 1.0
    for h in range(2):
        for p in range(64):
            pf32[p, 1 + h, 64 * h + p] = 1.0
    return cbf, cf32, pbf, pf32


def host_lvec(NL, b1, b2, lng, lnb, qsl, ksl):
    lv = np.zeros((NL, 128, SL_NV), np.float32)
    for li in range(NL):
        for hc in range(22):
            rows = min(128, 2730 - 128 * hc)
            lv[li, :rows, SL_B1A + hc] = b1[li][128 * hc:128 * hc + rows]
            lv[li, :rows, SL_B1G + hc] = b1[li][2730 + 128 * hc:2730 + 128 * hc + rows]
        for c in range(8):
            lv[li, :, SL_B2 + c] = b2[li][128 * c:128 * (c + 1)]
            lv[li, :, SL_LNG + c] = lng[li][128 * c:128 * (c + 1)]
            lv[li, :, SL_LNB + c] = lnb[li][128 * c:128 * (c + 1)]
        lv[li, :, SL_QS] = np.tile(qsl[li], 2)
        lv[li, :, SL_KS] = np.tile(ksl[li], 2)
        lv[li, :, SL_EPS] = 1e-5
        lv[li, :, SL_TINY] = 1e-30
    return lv


def host_ovec(gam, bet):
    ov = np.zeros((128, 32), np.float32)
    for c in range(8):
        ov[:, OV_G + c] = gam[128 * c:128 * (c + 1)]
        ov[:, OV_B + c] = bet[128 * c:128 * (c + 1)]
    ov[:, OV_EPS] = 1e-5
    return ov


def host_core_inputs(xb, base, g, plan, NL, shared):
    S = plan.S
    W = g.W
    span = xb[base:base + W]
    eb, mz, gt = build_outer_data(plan, base)
    m = dict(shared)
    m.update({
        "xfull": fm(xb),
        "xcore": fm(np.ascontiguousarray(span)),
        "msk_in": host_masks(base, W // 128),
        "rott_in": host_rott(S, np.arange(base, base + W)),
        "ebin": eb, "mzin": mz, "gtin": gt,
    })
    return m


from concourse.bass_utils import run_bass_kernel_spmd

S_FULL = 4096
W_CORE = 3072
N_LAYERS = 6
BASE_L = 0
BASE_R = 1024
SPLIT = 2572

_G = Geo(W=W_CORE, margin=16, tile=512, n_layers=N_LAYERS)
_PLAN = OPlan(_G, S=S_FULL, baseL=BASE_L, baseR=BASE_R)


def kernel(L0_attn_x, L0_attn_ln_gamma, L0_attn_ln_beta, L0_attn_Wq,
           L0_attn_Wk, L0_attn_Wv, L0_attn_Wo, L0_ffn_Wqkv, L0_ffn_q_scale,
           L0_ffn_k_scale, L0_ffn_Wo, L0_ffn_ln_g, L0_ffn_ln_b, L0_ffn_W1,
           L0_ffn_b1, L0_ffn_W2, L0_ffn_b2):
    g, plan = _G, _PLAN
    x = np.ascontiguousarray(np.asarray(L0_attn_x, np.float32))
    cbf, cf32, pbf, pf32 = host_consts()
    shared = {
        "owq": np.ascontiguousarray(L0_attn_Wq, np.float32),
        "owk": np.ascontiguousarray(L0_attn_Wk, np.float32),
        "owv": np.ascontiguousarray(L0_attn_Wv, np.float32),
        "owo": np.ascontiguousarray(L0_attn_Wo, np.float32),
        "fqkv": np.ascontiguousarray(L0_ffn_Wqkv, np.float32),
        "fwo": np.ascontiguousarray(L0_ffn_Wo, np.float32),
        "fw1": np.ascontiguousarray(L0_ffn_W1, np.float32),
        "fw2": np.ascontiguousarray(L0_ffn_W2, np.float32),
        "lvin": host_lvec(N_LAYERS, np.asarray(L0_ffn_b1, np.float32),
                          np.asarray(L0_ffn_b2, np.float32),
                          np.asarray(L0_ffn_ln_g, np.float32),
                          np.asarray(L0_ffn_ln_b, np.float32),
                          np.asarray(L0_ffn_q_scale, np.float32),
                          np.asarray(L0_ffn_k_scale, np.float32)),
        "ovin": host_ovec(np.asarray(L0_attn_ln_gamma, np.float32),
                          np.asarray(L0_attn_ln_beta, np.float32)),
        "cbf_in": cbf, "cf32_in": cf32, "pbf_in": pbf, "pf32_in": pf32,
    }
    role = {}
    for base in (BASE_L, BASE_R):
        eb, mz, gt = build_outer_data(plan, base)
        role[base] = {
            "msk_in": host_masks(base, g.W // 128),
            "rott_in": host_rott(S_FULL, np.arange(base, base + g.W)),
            "ebin": eb, "mzin": mz, "gtin": gt,
        }
    in_maps = []
    for b in range(4):
        xb = x[b]
        xf = fm(xb)
        for base in (BASE_L, BASE_R):
            m = dict(shared)
            m.update(role[base])
            m["xfull"] = xf
            m["xcore"] = np.ascontiguousarray(xf[:, :, base:base + g.W])
            in_maps.append(m)

    nc = build_full(g, plan, N_LAYERS)
    res = run_bass_kernel_spmd(nc, in_maps, core_ids=list(range(8)))

    out = np.empty((4, S_FULL, 1024), np.float32)
    for b in range(4):
        outL = unfm(np.asarray(res.results[2 * b]["outd"]))
        outR = unfm(np.asarray(res.results[2 * b + 1]["outd"]))
        out[b, 0:SPLIT] = outL[0:SPLIT]
        out[b, SPLIT:S_FULL] = outR[SPLIT - BASE_R:S_FULL - BASE_R]
    return out
